# version 10
# baseline (speedup 1.0000x reference)
import numpy as np
import concourse.bass as bass
import concourse.mybir as mybir
from concourse.bass_utils import run_bass_kernel_spmd

F32 = mybir.dt.float32
BF16 = mybir.dt.bfloat16
AF = mybir.ActivationFunctionType
ALU = mybir.AluOpType
AX = mybir.AxisListType

HID = 2048
INTER = 4096
NH = 64
HP = 64
NS = 128
KCONV = 4
CH = 256
CONV_DIM = INTER + 2 * NS
PROJ = INTER + CONV_DIM + NH
FMOE = 1408
FSH = 5632
NEXP = 8
EPS_LN = 1e-5
EPS_RMS = 1e-5

T = 1024
NCH = 4
NTS = 8
KT = HID // 128
KI = INTER // 128
GW = 256
NGRP = CONV_DIM // GW


class Ctr:
    def __init__(self, h):
        self.h = h
        self.n = 0

    def inc(self, k=1):
        self.n += k
        return self.n


class DmaPool:

    def __init__(self, handles):
        self.hs = list(handles)
        self.cnt = [0] * len(self.hs)
        self.rr = 0

    def grp(self, eng):
        i = self.rr
        self.rr = (self.rr + 1) % len(self.hs)
        if self.cnt[i] > 0:
            eng.wait_ge(self.hs[i], self.cnt[i])
        return i

    def tick(self, i, k=16):
        self.cnt[i] += k
        return (self.hs[i], self.cnt[i])

    def mark(self, i):
        return (self.hs[i], self.cnt[i])

    def finals(self):
        return [(h, c) for h, c in zip(self.hs, self.cnt) if c > 0]


def wq(eng, mark):
    eng.wait_ge(mark[0], mark[1])


def kernel(**inputs):
    in_maps = host_prep(inputs)
    nc, _ = build_nc()
    res = run_bass_kernel_spmd(nc, in_maps, core_ids=list(range(8)))
    out = np.empty((4, 2048, HID), np.float32)
    for c in range(8):
        b, hf = c // 2, c % 2
        out[b, hf * T:(hf + 1) * T, :] = res.results[c]["out"]
    return out


def host_prep(inputs):
    f32 = lambda k: np.ascontiguousarray(np.asarray(inputs[k], np.float32))
    x = f32("L0_attn_input_states")
    cT = np.ascontiguousarray(f32("L0_c").T)
    wmod = f32("L0_wmod")
    shared = dict(
        w_inT=np.ascontiguousarray(f32("L0_attn_in_proj_w").T),
        w_outT=np.ascontiguousarray(f32("L0_attn_out_proj_w").T),
        convw=np.ascontiguousarray(f32("L0_attn_conv_w").reshape(34, 128, KCONV)
                                   .transpose(1, 0, 2).reshape(128, 34 * KCONV)),
        convb=np.ascontiguousarray(f32("L0_attn_conv_b").reshape(34, 128).T),
        dtb=np.ascontiguousarray(f32("L0_attn_dt_bias").reshape(1, NH)),
        alog=np.ascontiguousarray(f32("L0_attn_A_log").reshape(1, NH)),
        dvec=np.ascontiguousarray(f32("L0_attn_D").reshape(1, NH)),
        normw=np.ascontiguousarray(f32("L0_attn_norm_w").reshape(1, INTER)),
        wr=np.ascontiguousarray(f32("L0_ffn_Wr").reshape(KT, 128, NEXP)
                                .transpose(1, 0, 2).reshape(128, KT * NEXP)),
        wsg=np.ascontiguousarray(f32("L0_ffn_Wsg").reshape(KT, 128).T),
        wgT=np.ascontiguousarray(f32("L0_ffn_Wg").reshape(NEXP, KT, 128, 11, 128)
                                 .transpose(0, 3, 2, 1, 4).reshape(NEXP, 11, 128, KT * 128)),
        wuT=np.ascontiguousarray(f32("L0_ffn_Wu").reshape(NEXP, KT, 128, 11, 128)
                                 .transpose(0, 3, 2, 1, 4).reshape(NEXP, 11, 128, KT * 128)),
        sgT=np.ascontiguousarray(f32("L0_ffn_Sg").reshape(KT, 128, 44, 128)
                                 .transpose(2, 1, 0, 3).reshape(44, 128, KT * 128)),
        suT=np.ascontiguousarray(f32("L0_ffn_Su").reshape(KT, 128, 44, 128)
                                 .transpose(2, 1, 0, 3).reshape(44, 128, KT * 128)),
        wd=f32("L0_ffn_Wd"), sd=f32("L0_ffn_Sd"),
    )
    in_maps = []
    for c in range(8):
        b, hf = c // 2, c % 2
        if hf == 1:
            xpre = np.ascontiguousarray(x[b, T - 3:T, :])
            premask = np.ones((4, 1), np.float32)
            msend = np.zeros((128, 1), np.float32)
            mrecv = np.ones((128, 1), np.float32)
        else:
            xpre = np.zeros((3, HID), np.float32)
            premask = np.zeros((4, 1), np.float32)
            msend = np.ones((128, 1), np.float32)
            mrecv = np.zeros((128, 1), np.float32)
        maskb = np.zeros((4, 1), np.float32)
        maskb[b, 0] = 1.0
        cstf = np.broadcast_to(np.array([EPS_LN, 1.0 / HID, -1.0 / HID, -0.5,
                                         EPS_RMS, 1.0 / INTER, 0.0, 0.0], np.float32),
                               (128, 8)).copy()
        m = dict(
            x=np.ascontiguousarray(x[b, hf * T:(hf + 1) * T, :]),
            xpre=xpre, premask=premask, msend=msend, mrecv=mrecv, maskb=maskb, cstf=cstf,
            ctsl=np.ascontiguousarray(cT[c * 256:(c + 1) * 256, :].reshape(2, 128, 4)
                                      .transpose(1, 0, 2).reshape(128, 8)),
            wmodsl=np.ascontiguousarray(wmod[c * 256:(c + 1) * 256, :]),
        )
        m.update(shared)
        in_maps.append(m)
    return in_maps


def build_nc(taps=(), upto="E"):
    nc = bass.Bass("TRN2", target_bir_lowering=False, debug=False)
    taps = set(taps)
    tap_d = {}

    def din(name, shape):
        return nc.dram_tensor(name, list(shape), F32, kind="ExternalInput").ap()

    x_d = din("x", (T, HID))
    xpre_d = din("xpre", (3, HID))
    premask_d = din("premask", (4, 1))
    msend_d = din("msend", (128, 1))
    mrecv_d = din("mrecv", (128, 1))
    maskb_d = din("maskb", (4, 1))
    cstf_d = din("cstf", (128, 8))
    ctsl_d = din("ctsl", (128, 8))
    wmodsl_d = din("wmodsl", (256, 12288))
    w_inT_d = din("w_inT", (HID, PROJ))
    w_outT_d = din("w_outT", (INTER, HID))
    convw_d = din("convw", (128, 34 * KCONV))
    convb_d = din("convb", (128, 34))
    dtb_d = din("dtb", (1, NH))
    alog_d = din("alog", (1, NH))
    dvec_d = din("dvec", (1, NH))
    normw_d = din("normw", (1, INTER))
    wr_d = din("wr", (128, KT * NEXP))
    wsg_d = din("wsg", (128, KT))
    wgT_d = din("wgT", (NEXP, 11, 128, KT * 128))
    wuT_d = din("wuT", (NEXP, 11, 128, KT * 128))
    wd_d = din("wd", (NEXP, FMOE, HID))
    sgT_d = din("sgT", (44, 128, KT * 128))
    suT_d = din("suT", (44, 128, KT * 128))
    sd_d = din("sd", (FSH, HID))
    out_d = nc.dram_tensor("out", [T, HID], F32, kind="ExternalOutput").ap()

    def mktap(name, shape):
        if name in taps:
            tap_d[name] = nc.dram_tensor("tap_" + name, list(shape), F32, kind="ExternalOutput").ap()
            return tap_d[name]
        return None

    t_h = mktap("h", (T, HID))
    t_hs = mktap("hs", (T, INTER))
    t_bc = mktap("bc", (2 * NS, T))
    t_dt = mktap("dt", (T, NH))
    t_y = mktap("y", (T, INTER))
    t_yn = mktap("yn", (T, INTER))
    t_c = mktap("c", (T, NEXP))
    t_hc = mktap("hc", (128, KT * T))

    w_inT3 = w_inT_d.rearrange("(k p) f -> p k f", p=128)
    w_outT3 = w_outT_d.rearrange("(i p) f -> p i f", p=128)
    mod_dram = nc.dram_tensor("mod_dram", [1, 6 * HID], F32, kind="Internal").ap()
    xnew_dram = nc.dram_tensor("xnew_dram", [T, HID], F32, kind="Internal").ap()
    ccin_mod = nc.dram_tensor("ccin_mod", [4, 6 * HID], F32, kind="Internal").ap()
    ccout_mod = nc.dram_tensor("ccout_mod", [4, 6 * HID], F32, kind="Internal").ap()
    ccin_st = nc.dram_tensor("ccin_st", [128, NH * HP], F32, kind="Internal").ap()
    ccout_st = nc.dram_tensor("ccout_st", [128, NH * HP], F32, kind="Internal").ap()

    SBASE = 16512

    def sat(name, offset, shape, dtype, **kw):
        return nc.alloc_sbuf_tensor_at(name, list(shape), dtype, offset=SBASE + offset, **kw).ap()

    yb = sat("yb", 0, [128, NTS * INTER], BF16)
    wmod_sb = sat("wmod_sb", 0, [128, 2 * 12288], BF16)
    modrow = sat("modrow", 0, [1, 6 * HID], F32)
    wot = sat("wot", 0, [128, 2 * KI * 256], BF16)
    moe = sat("moe", 0, [128, NTS * HID], F32)
    B2 = 65536
    hs_tm = sat("hs_tm", B2, [128, 2 * INTER], BF16)
    h_fm = sat("h_fm", B2 + 16384, [128, KT * 256], BF16)
    wtg = sat("wtg", B2 + 24576, [128, 2 * KT * GW], BF16)
    modpart = sat("modpart", B2, [4, 12288], F32)
    modfull = sat("modfull", B2, [4, 4096], F32)
    sstage = sat("sstage", B2, [128, NH * HP], F32)
    hfmC = sat("hfmC", B2, [128, KT * T], BF16)
    wtgC = sat("wtgC", B2 + 32768, [128, 2 * KT * 256], BF16)
    ssqp = sat("ssqp", B2 + 49152, [128, 128], F32)
    ynf = sat("ynf", B2, [128, KI * T], BF16)
    h2f = sat("h2f", B2, [128, KT * T], BF16)
    gu = sat("gu", B2 + 32768, [128, 11 * T], BF16)
    sgt = sat("sgt", B2 + 55296, [128, T], BF16)
    SOFF = 131072
    S = sat("S", SOFF, [128, NH * HP], F32)
    BCO = SOFF + 16384
    bcP0 = sat("bcP0", BCO, [128, HID], BF16)
    bcP1 = sat("bcP1", BCO + 4096, [128, HID], BF16)
    bcP2 = sat("bcP2", BCO + 8192, [128, HID], BF16)
    SC = BCO + 12288
    xt = sat("xt", SC, [128, HID], F32)
    htm = sat("htm", SC + 8192, [128, HID], F32)
    scr2 = sat("scr2", SC + 16384, [128, 1024], F32)
    r2B = sat("r2B", SC + 16384, [128, 512], F32)
    e_sbB = sat("e_sbB", SC + 16384 + 2048, [128, 512], BF16)
    mt_sbB = sat("mt_sbB", SC + 16384 + 3072, [128, 512], BF16)
    hs_ssdB = sat("hs_ssdB", BCO + 8192, [128, 128], BF16)
    bdecB = sat("bdecB", BCO + 8192 + 256, [128, 256], BF16)
    s16tB = sat("s16tB", BCO + 8192 + 768, [128, HP], BF16)
    yoB = sat("yoB", BCO + 8192 + 896, [128, 128], F32)
    y1B = sat("y1B", BCO + 8192 + 1408, [128, 128], F32)
    WS = SC + 20480
    wgt2 = sat("wgt2", WS, [128, 2 * KT * 128], BF16)
    wut2 = sat("wut2", WS + 8192, [128, 2 * KT * 128], BF16)
    normwB = sat("normwB", WS, [128, INTER], BF16)
    wdq0 = sat("wdq0", SOFF, [128, 11 * 512], BF16)
    wdq1 = sat("wdq1", SC, [128, 11 * 512], BF16)
    SM = WS + 22528
    _off = [SM]

    def sal(name, shape, dtype):
        t = sat(name, _off[0], shape, dtype)
        nbytes = int(np.prod(shape[1:])) * (2 if dtype == BF16 else 4)
        _off[0] += (nbytes + 31) // 32 * 32
        return t

    _offab = [B2 + 40960]

    def sab(name, shape, dtype):
        t = sat(name, _offab[0], shape, dtype)
        nbytes = int(np.prod(shape[1:])) * (2 if dtype == BF16 else 4)
        _offab[0] += (nbytes + 31) // 32 * 32
        return t

    cm_fm = sab("cm_fm", [128, T], BF16)
    bm_fm = sab("bm_fm", [128, 256], BF16)
    bm_tm = sab("bm_tm", [128, 256], BF16)
    sdo = sab("sdo", [128, 512], F32)
    acum = sab("acum", [128, 128], F32)
    acum_hm = sab("acum_hm", [64, 256], F32)
    dt_tm = sab("dt_tm", [128, 128], F32)
    adt_tm = sab("adt_tm", [128, 128], F32)
    trow = sab("trow", [1, 256], F32)
    xctail = sab("xctail", [128, 102], BF16)
    mask01 = sab("mask01", [128, 512], BF16)
    e_sb = sab("e_sb", [128, 512], BF16)
    gmm = sab("gmm", [128, 512], BF16)
    mt_sb = sab("mt_sb", [128, 512], BF16)
    hs_ssd = sab("hs_ssd", [128, 128], BF16)
    bdec = sab("bdec", [128, 256], BF16)
    decayb = sab("decayb", [128, 128], F32)
    dsub = sab("dsub", [128, 128], F32)
    yo = sab("yo", [128, 128], F32)
    y1 = sab("y1", [128, 128], F32)
    r2 = sab("r2", [128, 512], F32)
    s16t = sab("s16t", [128, HP], BF16)
    wdtT = sab("wdtT", [128, KT * NH], BF16)
    hpre = sat("hpre", 49152, [3, HID], F32)
    hpre_fm = sab("hpre_fm", [128, KT * 3], BF16)
    dts1 = sab("dts1", [128, 64], F32)
    dts2 = sab("dts2", [128, 64], F32)
    xc_sb = sab("xc_sb", [128, 2 * 260], BF16)
    diag = sab("diag", [128, 8 * 128], BF16)
    convw_sb = sab("convw_sb", [128, 34 * KCONV], F32)
    convb_sb = sab("convb_sb", [128, 34], F32)
    assert _offab[0] <= B2 + 65536, f"AB zone overflow: {_offab[0] - B2}"

    dtbB = sal("dtbB", [128, NH], F32)
    AB = sal("AB", [128, NH], F32)
    DB = sal("DB", [128, NH], F32)
    tcB = sal("tcB", [128, NH], F32)
    etcB = sal("etcB", [128, NH], F32)
    iden = sal("iden", [128, 128], BF16)
    idf = sal("idf", [128, 128], F32)
    t01f = sal("t01f", [128, 128], F32)
    onesf = sal("onesf", [128, 128], F32)
    c_sb = sal("c_sb", [128, NEXP * NTS], F32)
    sig_sb = sal("sig_sb", [128, NTS], F32)
    st8 = sal("st8", [128, 16], F32)
    silu_ct = sal("silu_ct", [128, 8], BF16)
    maskb_sb = sal("maskb_sb", [4, 1], F32)
    premask_sb = sal("premask_sb", [4, 1], F32)
    msend_sb = sal("msend_sb", [128, 1], F32)
    mrecv_sb = sal("mrecv_sb", [128, 1], F32)
    wr_sb = sal("wr_sb", [128, KT * NEXP], BF16)
    wsg_sb = sal("wsg_sb", [128, KT], BF16)
    rowst = sal("rowst", [1, 512], F32)
    cst = sal("cst", [128, 8], F32)
    evrow = sal("evrow", [1, 2 * NH], F32)
    assert _off[0] <= 224 * 1024, f"SBUF overflow: {_off[0]}"

    pb = [nc.alloc_psum_tensor(f"pb{i}", [128, 512], F32).ap() for i in range(7)]
    pt0 = nc.alloc_psum_tensor("pt0", [128, 1024], BF16).ap()

    import contextlib
    with contextlib.ExitStack() as _stk:
        _qs_h = [_stk.enter_context(nc.semaphore(f"qs{i}")) for i in range(8)]
        _qg_h = [_stk.enter_context(nc.semaphore(f"qg{i}")) for i in range(8)]
        _tE = _stk.enter_context(nc.semaphore("tE"))
        _vE = _stk.enter_context(nc.semaphore("vE"))
        _aE = _stk.enter_context(nc.semaphore("aE"))
        _gp = _stk.enter_context(nc.semaphore("gp"))
        _cc = _stk.enter_context(nc.semaphore("cc"))
        block = _stk.enter_context(nc.Block())
        QS, QG = DmaPool(_qs_h), DmaPool(_qg_h)
        tE, vE, aE, gp = Ctr(_tE), Ctr(_vE), Ctr(_aE), Ctr(_gp)
        cc = Ctr(_cc)
        M = {}

        def vss(inst, v):
            inst.then_inc(_vE, 1)
            v.wait_ge(_vE, vE.inc())

        def ass(inst, sc):
            inst.then_inc(_aE, 1)
            sc.wait_ge(_aE, aE.inc())

        def p0_gpsimd(g):
            gi = QG.grp(g)
            g.dma_start(out=wmod_sb[:, 0:12288], in_=wmodsl_d[0:128, :]).then_inc(QG.hs[gi], 16)
            QG.tick(gi)
            g.dma_start(out=wmod_sb[:, 12288:24576], in_=wmodsl_d[128:256, :]).then_inc(QG.hs[gi], 16)
            QG.tick(gi)
            g.dma_start(out=silu_ct, in_=ctsl_d).then_inc(QG.hs[gi], 16)
            M["wmod"] = QG.tick(gi)
            gi = QG.grp(g)
            for dst, src in ((maskb_sb, maskb_d), (premask_sb, premask_d), (cst, cstf_d),
                             (msend_sb, msend_d), (mrecv_sb, mrecv_d),
                             (convb_sb, convb_d),
                             (convw_sb, convw_d),
                             (scr2[0:1, 0:64], dtb_d), (scr2[0:1, 64:128], alog_d),
                             (scr2[0:1, 128:192], dvec_d), (hpre, xpre_d),
                             (wr_sb, wr_d),
                             (wsg_sb, wsg_d)):
                g.dma_start(out=dst, in_=src).then_inc(QG.hs[gi], 16)
                QG.tick(gi)
            M["smalls"] = QG.mark(gi)
            g.memset(onesf, 1.0)
            g.memset(iden, 1.0)
            g.memset(t01f, 1.0)
            g.memset(mask01, 1.0)
            g.memset(evrow, 0.0)
            g.memset(S, 0.0).then_inc(_gp, 1)
            g.wait_ge(_gp, gp.inc())
            g.affine_select(iden, iden, pattern=[[1, 128]], compare_op=ALU.is_equal,
                            fill=0.0, base=0, channel_multiplier=-1)
            g.affine_select(t01f, t01f, pattern=[[1, 128]], compare_op=ALU.is_ge,
                            fill=0.0, base=0, channel_multiplier=-1)
            g.affine_select(mask01[:, 0:256], mask01[:, 0:256], pattern=[[1, 256]],
                            compare_op=ALU.is_ge, fill=0.0, base=0, channel_multiplier=-1)
            g.affine_select(mask01[:, 256:512], mask01[:, 256:512], pattern=[[1, 256]],
                            compare_op=ALU.is_ge, fill=0.0, base=-128,
                            channel_multiplier=-1).then_inc(_gp, 1)
            g.wait_ge(_gp, gp.inc())
            g.tensor_copy(idf, iden).then_inc(_gp, 1)
            M["consts"] = gp.inc()
            gi = QG.grp(g)
            for kq in range(4):
                g.dma_start(out=wdtT[:, kq * 4 * NH:(kq + 1) * 4 * NH].rearrange(
                                "p (k f) -> p k f", k=4),
                            in_=w_inT3[:, kq * 4:(kq + 1) * 4, INTER + CONV_DIM:PROJ]
                            ).then_inc(QG.hs[gi], 16)
                QG.tick(gi)
            M["wdtT"] = QG.mark(gi)

        block.gpsimd(p0_gpsimd)

        def p0_scalar(sc):
            wq(sc, M["wmod"])
            ass(sc.activation(silu_ct, silu_ct, AF.Silu), sc)
            M["siluct"] = aE.n
            wq(sc, M["smalls"])
            ass(sc.activation(scr2[0:1, 64:128], scr2[0:1, 64:128], AF.Exp), sc)
            M["expalog"] = aE.n

        block.scalar(p0_scalar)

        for j in range(24):
            def _mt(t, j=j):
                if j == 0:
                    t.wait_ge(_aE, M["siluct"])
                bk = pb[j % 2]
                if j >= 2:
                    t.wait_ge(_vE, M[("modev", j - 2)])
                for a in range(2):
                    mm = t.matmul(bk[0:4, :], silu_ct[:, 4 * a:4 * a + 4],
                                  wmod_sb[:, a * 12288 + j * 512: a * 12288 + (j + 1) * 512],
                                  start=(a == 0), stop=(a == 1))
                mm.then_inc(_tE, 1)
                M[("modmm", j)] = tE.inc()

            def _mv(v, j=j):
                v.wait_ge(_tE, M[("modmm", j)])
                v.tensor_copy(modpart[:, j * 512:(j + 1) * 512], pb[j % 2][0:4, :]).then_inc(_vE, 1)
                M[("modev", j)] = vE.inc()

            block.tensor(_mt)
            block.vector(_mv)
        M["modpart_done"] = vE.n

        def p0_cc(g):
            g.wait_ge(_vE, M["modpart_done"])
            gi = QG.grp(g)
            g.dma_start(out=ccin_mod, in_=modpart).then_inc(QG.hs[gi], 16)
            wq(g, QG.tick(gi))
            g.collective_compute("AllReduce", ALU.add, replica_groups=[list(range(8))],
                                 ins=[ccin_mod], outs=[ccout_mod]).then_inc(_cc, 1)
            cc.inc()
            g.wait_ge(_cc, cc.n)

        block.gpsimd(p0_cc)

        for part in range(3):
            def _ld(g, part=part):
                if part > 0:
                    g.wait_ge(_vE, M[("rowev", part - 1, 7)])
                gi = QG.grp(g)
                g.dma_start(out=modfull[:, 0:4096],
                            in_=ccout_mod[:, part * 4096:(part + 1) * 4096]).then_inc(QG.hs[gi], 16)
                M[("modfull", part)] = QG.tick(gi)

            block.gpsimd(_ld)

            for j in range(8):
                def _mm(t, part=part, j=j):
                    if j == 0:
                        wq(t, M[("modfull", part)])
                        if part == 0:
                            wq(t, M["smalls"])
                    bk = pb[2 + (j % 2)]
                    if j >= 2:
                        t.wait_ge(_vE, M[("rowev", part, j - 2)])
                    elif part > 0:
                        t.wait_ge(_vE, M[("rowev", part - 1, 6 + j)])
                    t.matmul(bk[0:1, :], maskb_sb, modfull[:, j * 512:(j + 1) * 512],
                             start=True, stop=True).then_inc(_tE, 1)
                    M[("rowmm", part, j)] = tE.inc()

                def _ev(v, part=part, j=j):
                    v.wait_ge(_tE, M[("rowmm", part, j)])
                    v.tensor_copy(modrow[:, part * 4096 + j * 512: part * 4096 + (j + 1) * 512],
                                  pb[2 + (j % 2)][0:1, :]).then_inc(_vE, 1)
                    M[("rowev", part, j)] = vE.inc()

                block.tensor(_mm)
                block.vector(_ev)
        M["modrow_done"] = vE.n

        def p0c_vector(v):
            v.wait_ge(_vE, M["modrow_done"])
            for base in (2 * HID, 5 * HID):
                gaap = modrow[:, base:base + HID]
                vss(v.tensor_scalar(xt[0:1, :], gaap, 0.0, None, ALU.is_ge), v)
                vss(v.tensor_scalar(xt[0:1, :], xt[0:1, :], 2.0, -1.0, ALU.mult, ALU.add), v)
                vss(v.tensor_scalar(htm[0:1, :], gaap, -1.0, None, ALU.mult), v)
                vss(v.tensor_max(htm[0:1, :], htm[0:1, :], gaap), v)
                vss(v.tensor_scalar_max(htm[0:1, :], htm[0:1, :], 0.25), v)
                vss(v.tensor_mul(gaap, xt[0:1, :], htm[0:1, :]), v)
            M["clamped"] = vE.n

        block.vector(p0c_vector)

        def p0c_sync(s):
            s.wait_ge(_vE, M["clamped"])
            gi = QS.grp(s)
            s.dma_start(out=mod_dram, in_=modrow).then_inc(QS.hs[gi], 16)
            M["mod_dram"] = QS.tick(gi)

        block.sync(p0c_sync)

        def emit_bcast(tag, dst, src_row, ncols, add1=False, bank=6, pre_tmark=None):
            nj = (ncols + 511) // 512
            prev_ev = vE.n
            for j in range(nj):
                def _t(t, j=j):
                    if j == 0:
                        t.wait_ge(_vE, prev_ev)
                        if pre_tmark is not None:
                            for mm in pre_tmark:
                                t.wait_ge(mm[0], mm[1])
                    w = min(512, ncols - j * 512)
                    if j >= 1:
                        t.wait_ge(_vE, M[(tag, "ev", j - 1)])
                    t.matmul(pb[bank][:, 0:w], onesf[0:1, :], src_row[:, j * 512:j * 512 + w],
                             start=True, stop=True).then_inc(_tE, 1)
                    M[(tag, "mm", j)] = tE.inc()

                def _v(v, j=j):
                    w = min(512, ncols - j * 512)
                    v.wait_ge(_tE, M[(tag, "mm", j)])
                    if add1:
                        v.tensor_scalar(dst[:, j * 512:j * 512 + w], pb[bank][:, 0:w],
                                        1.0, None, ALU.add).then_inc(_vE, 1)
                    else:
                        v.tensor_copy(dst[:, j * 512:j * 512 + w], pb[bank][:, 0:w]).then_inc(_vE, 1)
                    M[(tag, "ev", j)] = vE.inc()

                block.tensor(_t)
                block.vector(_v)
            M[tag] = vE.n

        emit_bcast("bc_ca", bcP0, modrow[:, HID:2 * HID], HID, add1=True,
                   pre_tmark=[(_vE, M["clamped"]), (_gp, M["consts"])])
        emit_bcast("bc_sa", bcP1, modrow[:, 0:HID], HID, add1=False)

        def p0d_tensor2(t):
            t.wait_ge(_aE, M["expalog"])
            t.wait_ge(_vE, M["bc_sa"])
            t.matmul(pb[6][:, 0:64], onesf[0:1, :], scr2[0:1, 0:64], start=True, stop=True)
            t.matmul(pb[6][:, 64:128], onesf[0:1, :], scr2[0:1, 64:128], start=True, stop=True)
            t.matmul(pb[6][:, 128:192], onesf[0:1, :], scr2[0:1, 128:192],
                     start=True, stop=True).then_inc(_tE, 1)
            M["rows3"] = tE.inc()

        def p0d_vector2(v):
            v.wait_ge(_tE, M["rows3"])
            v.tensor_copy(dtbB, pb[6][:, 0:64])
            v.tensor_scalar(AB, pb[6][:, 64:128], -1.0, None, ALU.mult)
            v.tensor_copy(DB, pb[6][:, 128:192]).then_inc(_vE, 1)
            M["rows3v"] = vE.inc()

        block.tensor(p0d_tensor2)
        block.vector(p0d_vector2)

        def emit_ln(tag, vsrc, n_part, dst_tm, waits_v=()):
            def _v(v):
                for wk in waits_v:
                    v.wait_ge(wk[0], wk[1])
                v.reduce_sum(st8[0:n_part, 0:1], vsrc, axis=AX.X)
                v.scalar_tensor_tensor(dst_tm, vsrc, 1.0, vsrc, ALU.mult, ALU.mult,
                                       accum_out=st8[0:n_part, 1:2]).then_inc(_vE, 1)
                M[(tag, "stat")] = vE.inc()

            def _s(sc):
                sc.wait_ge(_vE, M[(tag, "stat")])
                wq(sc, M["smalls"])
                sc.activation(st8[0:n_part, 2:3], st8[0:n_part, 0:1], AF.Square,
                              scale=cst[0:n_part, 1:2])
                sc.activation(st8[0:n_part, 3:4], st8[0:n_part, 0:1], AF.Copy,
                              scale=cst[0:n_part, 2:3]).then_inc(_aE, 1)
                M[(tag, "sc")] = aE.inc()

            def _v2(v):
                v.wait_ge(_aE, M[(tag, "sc")])
                v.tensor_scalar(st8[0:n_part, 4:5], st8[0:n_part, 1:2], 1.0 / HID,
                                st8[0:n_part, 2:3], ALU.mult, ALU.subtract).then_inc(_vE, 1)
                M[(tag, "var")] = vE.inc()

            def _s2(sc):
                sc.wait_ge(_vE, M[(tag, "var")])
                ass(sc.activation(st8[0:n_part, 5:6], st8[0:n_part, 4:5], AF.Ln,
                                  bias=cst[0:n_part, 0:1]), sc)
                sc.activation(st8[0:n_part, 6:7], st8[0:n_part, 5:6], AF.Exp,
                              scale=cst[0:n_part, 3:4]).then_inc(_aE, 1)
                M[(tag, "std")] = aE.inc()

            def _v3(v):
                v.wait_ge(_aE, M[(tag, "std")])
                vss(v.tensor_scalar(vsrc, vsrc, st8[0:n_part, 3:4], st8[0:n_part, 6:7],
                                    ALU.add, ALU.mult), v)
                vss(v.tensor_mul(dst_tm, vsrc, bcP0[0:n_part, :]), v)
                v.tensor_add(dst_tm, dst_tm, bcP1[0:n_part, :]).then_inc(_vE, 1)
                M[(tag, "h")] = vE.inc()

            block.vector(_v)
            block.scalar(_s)
            block.vector(_v2)
            block.scalar(_s2)
            block.vector(_v3)

        def p0e(g=None):
            emit_ln("pre", hpre, 3, htm[0:3, :], waits_v=[(_vE, M["bc_ca"]),
                                                          (_vE, M["bc_sa"]), M["smalls"]])

            def _vm(v):
                v.wait_ge(_vE, M[("pre", "h")])
                v.tensor_scalar(hpre, hpre, premask_sb[0:3, 0:1], None, ALU.mult).then_inc(_vE, 1)
                M["hpre_m"] = vE.inc()

            block.vector(_vm)

            def _t(t):
                t.wait_ge(_vE, M["hpre_m"])
                for kb in range(KT):
                    t.transpose(pb[4][:, (kb % 4) * 3:(kb % 4) * 3 + 3],
                                hpre[0:3, kb * 128:(kb + 1) * 128], idf[0:3, 0:3])
                    if kb % 4 == 3:
                        t.nop().then_inc(_tE, 1)
                        M[("preT", kb // 4)] = tE.inc()
                        if kb < KT - 1:
                            t.wait_ge(_vE, M[("preTev", kb // 4)] if False else 0)

            block.tensor(_t)

            def _t2v():
                for q in range(4):
                    def _v(v, q=q):
                        v.wait_ge(_tE, M[("preT", q)])
                        for kb in range(q * 4, q * 4 + 4):
                            v.tensor_copy(hpre_fm[:, kb * 3:(kb + 1) * 3],
                                          pb[4][:, (kb % 4) * 3:(kb % 4) * 3 + 3])
                        v.nop().then_inc(_vE, 1)
                        M[("preTev", q)] = vE.inc()
                    block.vector(_v)

            _t2v()

        def p0e_fixed():
            emit_ln("pre", hpre, 3, htm[0:3, :], waits_v=[(_vE, M["bc_ca"]),
                                                          (_vE, M["bc_sa"])])

            def _vm(v):
                v.tensor_scalar(hpre, hpre, premask_sb[0:3, 0:1], None, ALU.mult).then_inc(_vE, 1)
                M["hpre_m"] = vE.inc()

            block.vector(_vm)
            for q in range(4):
                def _t(t, q=q):
                    if q == 0:
                        t.wait_ge(_vE, M["hpre_m"])
                    else:
                        t.wait_ge(_vE, M[("preTev", q - 1)])
                    for kk in range(4):
                        kb = q * 4 + kk
                        li = t.transpose(pb[4][:, kk * 3:kk * 3 + 3],
                                         hpre[0:3, kb * 128:(kb + 1) * 128], idf[0:3, 0:3])
                    li.then_inc(_tE, 1)
                    M[("preT", q)] = tE.inc()

                def _v(v, q=q):
                    v.wait_ge(_tE, M[("preT", q)])
                    for kk in range(4):
                        kb = q * 4 + kk
                        li = v.tensor_copy(hpre_fm[:, kb * 3:(kb + 1) * 3],
                                           pb[4][:, kk * 3:kk * 3 + 3])
                    li.then_inc(_vE, 1)
                    M[("preTev", q)] = vE.inc()

                block.tensor(_t)
                block.vector(_v)

        p0e_fixed()
        M["hpre_fm"] = vE.n

        def emit_stub_fin():
            def _fin0(s):
                s.wait_ge(_vE, vE.n)
                s.wait_ge(_aE, aE.n)
                s.wait_ge(_tE, tE.n)
                for mk in QS.finals() + QG.finals():
                    wq(s, mk)
                gi = QS.grp(s)
                for g2 in range(NTS):
                    s.dma_start(out=out_d[g2 * 128:(g2 + 1) * 128, :],
                                in_=onesf[:, 0:1].broadcast_to([128, HID])).then_inc(QS.hs[gi], 16)
                    QS.tick(gi)
                wq(s, QS.mark(gi))

            block.sync(_fin0)

        if upto == "P0":
            emit_stub_fin()
            return nc, tap_d

        def emit_hslice(tagpfx, g, dst_fm, fm_stride, x_src, extra_v_waits=(),
                        pe_free_mark=None, extra_s_waits=()):
            tag = (tagpfx, g)

            def _sy(s):
                if ("xtfree",) in M:
                    s.wait_ge(_vE, M[("xtfree",)])
                for wk in extra_s_waits:
                    wq(s, wk)
                gi = QS.grp(s)
                s.dma_start(out=xt, in_=x_src).then_inc(QS.hs[gi], 16)
                M[(tag, "x")] = QS.tick(gi)

            block.sync(_sy)
            waits = [M[(tag, "x")]] + list(extra_v_waits)
            if ("htmfree",) in M:
                waits.append((_tE, M[("htmfree",)]))
            emit_ln(tag, xt, 128, htm, waits_v=waits)
            M[("xtfree",)] = M[(tag, "h")]

            if t_h is not None and tagpfx == "hA":
                def _tap(s):
                    s.wait_ge(_vE, M[(tag, "h")])
                    gi = QS.grp(s)
                    s.dma_start(out=t_h[g * 128:(g + 1) * 128, :], in_=htm).then_inc(QS.hs[gi], 16)
                    M[(tag, "tap")] = QS.tick(gi)

                block.sync(_tap)

            for q in range(4):
                def _t(t, q=q):
                    if q == 0:
                        t.wait_ge(_vE, M[(tag, "h")])
                        if (tag, "tap") in M:
                            wq(t, M[(tag, "tap")])
                        if pe_free_mark is not None:
                            t.wait_ge(_vE, pe_free_mark)
                    bk = pb[5 if q % 2 == 0 else 6]
                    if (tag, "Tev", q - 2) in M:
                        t.wait_ge(_vE, M[(tag, "Tev", q - 2)])
                    elif ("pT_ev", 5 if q % 2 == 0 else 6) in M:
                        t.wait_ge(_vE, M.pop(("pT_ev", 5 if q % 2 == 0 else 6)))
                    for kk in range(4):
                        kb = q * 4 + kk
                        li = t.transpose(bk[:, kk * 128:(kk + 1) * 128],
                                         htm[:, kb * 128:(kb + 1) * 128], idf)
                    li.then_inc(_tE, 1)
                    M[(tag, "T", q)] = tE.inc()

                def _v(v, q=q):
                    v.wait_ge(_tE, M[(tag, "T", q)])
                    bk = pb[5 if q % 2 == 0 else 6]
                    for kk in range(4):
                        kb = q * 4 + kk
                        col = kb * fm_stride + (g % (fm_stride // 128)) * 128
                        li = v.tensor_copy(dst_fm[:, col:col + 128], bk[:, kk * 128:(kk + 1) * 128])
                    li.then_inc(_vE, 1)
                    M[(tag, "Tev", q)] = vE.inc()

                block.tensor(_t)
                block.vector(_v)
            M[("htmfree",)] = M[(tag, "T", 3)]
            M[("pT_ev", 5)] = M[(tag, "Tev", 2)]
            M[("pT_ev", 6)] = M[(tag, "Tev", 3)]
            return M[(tag, "Tev", 3)]

        def emit_chunk_inproj(icu):
            pfx = icu == 0
            for g in range(NGRP):
                buf = g % 2
                tagw = ("wtg", icu, g)

                def _stage(s, g=g, buf=buf, tagw=tagw):
                    key = ("wtgmm", icu, g - 2) if g >= 2 else \
                        (("wtgmm", icu - 1, 16 if buf == 0 else 15) if icu > 0 else None)
                    if key and key in M:
                        s.wait_ge(_tE, M[key])
                    r0 = 4096 + g * GW
                    gi = QG.grp(s)
                    for kq in range(4):
                        s.dma_start(out=wtg[:, buf * 4096 + kq * 4 * GW:
                                            buf * 4096 + (kq + 1) * 4 * GW].rearrange(
                                        "p (k f) -> p k f", k=4),
                                    in_=w_inT3[:, kq * 4:(kq + 1) * 4,
                                               r0:r0 + GW]).then_inc(QG.hs[gi], 16)
                        QG.tick(gi)
                    M[tagw] = QG.mark(gi)

                block.gpsimd(_stage)

                def _mm1(t, ct, buf=buf, tagw=tagw):
                    if ct % 2 == 0:
                        wq(t, M[tagw])
                    if ct == 0:
                        t.wait_ge(_vE, M[("hfm_done", icu)])
                        if pfx:
                            t.wait_ge(_vE, M["hpre_fm"])
                        if icu > 0:
                            t.wait_ge(_vE, M[("ssdv", icu - 1, NH - 1)])
                            t.wait_ge(_aE, M[("yoh", icu - 1, NH - 1)])
                    mt = ct % 2
                    bk = pb[ct % 2]
                    key = ("xcev", icu, ct - 2) if ct >= 2 else \
                        (("xcev", icu - 1, 32 + ct) if icu > 0 else None)
                    if key and key in M:
                        t.wait_ge(_vE, M[key])
                    for k in range(KT):
                        li = t.matmul(bk[:, 0:256],
                                      wtg[:, buf * 4096 + k * GW + mt * 128:
                                          buf * 4096 + k * GW + mt * 128 + 128],
                                      h_fm[:, k * 256:(k + 1) * 256],
                                      start=(k == 0), stop=(k == KT - 1))
                    if pfx:
                        if ("xcev", icu, ct - 1) in M:
                            t.wait_ge(_vE, M[("xcev", icu, ct - 1)])
                        for k in range(KT):
                            li = t.matmul(pb[4][:, 0:3],
                                          wtg[:, buf * 4096 + k * GW + mt * 128:
                                              buf * 4096 + k * GW + mt * 128 + 128],
                                          hpre_fm[:, k * 3:(k + 1) * 3],
                                          start=(k == 0), stop=(k == KT - 1))
                    li.then_inc(_tE, 1)
                    M[("xcmm", icu, ct)] = tE.inc()
                    if ct % 2 == 1:
                        M[("wtgmm", icu, g)] = tE.n

                def _evac1(v, ct):
                    bk = pb[ct % 2]
                    b = ct % 2
                    v.wait_ge(_tE, M[("xcmm", icu, ct)])
                    key = ("cvmm", icu, ct - 2) if ct >= 2 else \
                        (("cvmm", icu - 1, 32 + ct) if icu > 0 else None)
                    if key and key in M:
                        v.wait_ge(_tE, M[key])
                    if icu == 0 and ct == 0:
                        wq(v, M["smalls"])
                    if pfx:
                        v.tensor_copy(xc_sb[:, b * 260:b * 260 + 3], pb[4][:, 0:3])
                    else:
                        v.tensor_copy(xc_sb[:, b * 260:b * 260 + 3], xctail[:, ct * 3:ct * 3 + 3])
                    vss(v.tensor_copy(xc_sb[:, b * 260 + 3:b * 260 + 259], bk[:, 0:256]), v)
                    v.tensor_copy(xctail[:, ct * 3:ct * 3 + 3],
                                  xc_sb[:, b * 260 + 256:b * 260 + 259])
                    for k in range(KCONV):
                        li = v.tensor_scalar(diag[:, (b * 4 + k) * 128:(b * 4 + k + 1) * 128], iden,
                                             convw_sb[:, ct * KCONV + k:ct * KCONV + k + 1],
                                             None, ALU.mult)
                    li.then_inc(_vE, 1)
                    M[("xcev", icu, ct)] = vE.inc()

                def _conv1(t, ct):
                    b = ct % 2
                    bk = pb[2 + (ct % 2)]
                    t.wait_ge(_vE, M[("xcev", icu, ct)])
                    key = ("cvac", icu, ct - 2) if ct >= 2 else \
                        (("cvac", icu - 1, 32 + ct) if icu > 0 else None)
                    if key and key in M:
                        t.wait_ge(_aE, M[key])
                    for k in range(KCONV):
                        li = t.matmul(bk[:, 0:256], diag[:, (b * 4 + k) * 128:(b * 4 + k + 1) * 128],
                                      xc_sb[:, b * 260 + k:b * 260 + k + 256],
                                      start=(k == 0), stop=(k == KCONV - 1))
                    li.then_inc(_tE, 1)
                    M[("cvmm", icu, ct)] = tE.inc()

                def _cvevac1(sc, ct):
                    bk = pb[2 + (ct % 2)]
                    sc.wait_ge(_tE, M[("cvmm", icu, ct)])
                    if icu == 0 and ct == 0:
                        wq(sc, M["smalls"])
                    if ct < 32:
                        if ("hsT", icu, ct - 1) in M:
                            sc.wait_ge(_tE, M[("hsT", icu, ct - 1)])
                        elif icu > 0 and ("hsT", icu - 1, 31) in M:
                            sc.wait_ge(_tE, M[("hsT", icu - 1, 31)])
                        sc.activation(e_sb[:, 0:256], bk[:, 0:256], AF.Silu,
                                      bias=convb_sb[:, ct:ct + 1]).then_inc(_aE, 1)
                    elif ct == 32:
                        sc.activation(bm_fm[:, 0:256], bk[:, 0:256], AF.Silu,
                                      bias=convb_sb[:, ct:ct + 1]).then_inc(_aE, 1)
                    else:
                        sc.activation(cm_fm[:, icu * 256:(icu + 1) * 256], bk[:, 0:256],
                                      AF.Silu, bias=convb_sb[:, ct:ct + 1]).then_inc(_aE, 1)
                    M[("cvac", icu, ct)] = aE.inc()

                def _hsT1(t, ct):
                    if ct >= 32:
                        return
                    t.wait_ge(_aE, M[("cvac", icu, ct)])
                    if ("hsTev", icu, ct - 1) in M:
                        t.wait_ge(_vE, M[("hsTev", icu, ct - 1)])
                    elif icu > 0 and ("hsTev", icu - 1, 31) in M:
                        t.wait_ge(_vE, M[("hsTev", icu - 1, 31)])
                    for sl in range(2):
                        li = t.transpose(pt0[:, sl * 128:(sl + 1) * 128],
                                         e_sb[:, sl * 128:sl * 128 + 128], iden)
                    li.then_inc(_tE, 1)
                    M[("hsT", icu, ct)] = tE.inc()

                def _hsTev1(v, ct):
                    if ct >= 32:
                        return
                    v.wait_ge(_tE, M[("hsT", icu, ct)])
                    for sl in range(2):
                        li = v.tensor_copy(hs_tm[:, sl * INTER + ct * 128: sl * INTER + ct * 128 + 128],
                                           pt0[:, sl * 128:(sl + 1) * 128])
                    li.then_inc(_vE, 1)
                    M[("hsTev", icu, ct)] = vE.inc()

                c0, c1 = g * 2, g * 2 + 1
                if pfx:
                    block.tensor(lambda t, ct=c0: _mm1(t, ct))
                    block.vector(lambda v, ct=c0: _evac1(v, ct))
                    block.tensor(lambda t, ct=c0: _conv1(t, ct))
                    block.tensor(lambda t, ct=c1: _mm1(t, ct))
                    block.vector(lambda v, ct=c1: _evac1(v, ct))
                    block.tensor(lambda t, ct=c1: _conv1(t, ct))
                else:
                    block.tensor(lambda t, ct=c0: _mm1(t, ct))
                    block.tensor(lambda t, ct=c1: _mm1(t, ct))
                    block.vector(lambda v, ct=c0: _evac1(v, ct))
                    block.tensor(lambda t, ct=c0: _conv1(t, ct))
                    block.vector(lambda v, ct=c1: _evac1(v, ct))
                    block.tensor(lambda t, ct=c1: _conv1(t, ct))
                block.scalar(lambda sc, ct=c0: _cvevac1(sc, ct))
                block.tensor(lambda t, ct=c0: _hsT1(t, ct))
                block.vector(lambda v, ct=c0: _hsTev1(v, ct))
                block.scalar(lambda sc, ct=c1: _cvevac1(sc, ct))
                block.tensor(lambda t, ct=c1: _hsT1(t, ct))
                block.vector(lambda v, ct=c1: _hsTev1(v, ct))

            def _bT(t):
                t.wait_ge(_aE, M[("cvac", icu, 32)])
                t.wait_ge(_vE, M[("hsTev", icu, 31)])
                for sl in range(2):
                    li = t.transpose(pt0[:, sl * 128:(sl + 1) * 128],
                                     bm_fm[:, sl * 128:sl * 128 + 128], iden)
                li.then_inc(_tE, 1)
                M[("bT", icu)] = tE.inc()

            def _bTev(v):
                v.wait_ge(_tE, M[("bT", icu)])
                for sl in range(2):
                    li = v.tensor_copy(bm_tm[:, sl * 128:(sl + 1) * 128],
                                       pt0[:, sl * 128:(sl + 1) * 128])
                li.then_inc(_vE, 1)
                M[("bTev", icu)] = vE.inc()

            block.tensor(_bT)
            block.vector(_bTev)

            for tsl in range(2):
                def _dtmm(t, tsl=tsl):
                    if icu == 0 and tsl == 0:
                        wq(t, M["wdtT"])
                    key = ("dtv", icu, tsl - 1) if tsl == 1 else \
                        (("dtv", icu - 1, 1) if icu > 0 else None)
                    if key and key in M:
                        t.wait_ge(_vE, M[key])
                    if tsl == 0 and icu > 0:
                        t.wait_ge(_vE, M[("acv", icu - 1)])
                    if pfx:
                        t.wait_ge(_vE, M[("xcev", icu, 33)])
                    for k in range(KT):
                        li = t.matmul(pb[4][:, 0:64],
                                      h_fm[:, k * 256 + tsl * 128: k * 256 + tsl * 128 + 128],
                                      wdtT[:, k * NH:(k + 1) * NH],
                                      start=(k == 0), stop=(k == KT - 1))
                    li.then_inc(_tE, 1)
                    M[("dtmm", icu, tsl)] = tE.inc()

                block.tensor(_dtmm)

                def _dtv(v, tsl=tsl):
                    v.wait_ge(_tE, M[("dtmm", icu, tsl)])
                    if icu == 0 and tsl == 0:
                        v.wait_ge(_vE, M["rows3v"])
                    v.tensor_add(dts1, pb[4][:, 0:64], dtbB).then_inc(_vE, 1)
                    M[("dts1", icu, tsl)] = vE.inc()

                block.vector(_dtv)

                def _dta(sc, tsl=tsl):
                    sc.wait_ge(_vE, M[("dts1", icu, tsl)])
                    ass(sc.activation(dts2, dts1, AF.Abs), sc)
                    ass(sc.activation(dts2, dts2, AF.Exp, scale=-1.0), sc)
                    sc.activation(dts2, dts2, AF.Ln, bias=1.0).then_inc(_aE, 1)
                    M[("dta", icu, tsl)] = aE.inc()

                block.scalar(_dta)

                def _dtv2(v, tsl=tsl):
                    v.wait_ge(_aE, M[("dta", icu, tsl)])
                    vss(v.scalar_tensor_tensor(dt_tm[:, tsl * 64:(tsl + 1) * 64], dts1, 0.0, dts2,
                                               ALU.max, ALU.add), v)
                    v.tensor_scalar(dt_tm[:, tsl * 64:(tsl + 1) * 64],
                                    dt_tm[:, tsl * 64:(tsl + 1) * 64], 1e-3, 100.0,
                                    ALU.max, ALU.min).then_inc(_vE, 1)
                    M[("dtclip", icu, tsl)] = vE.inc()
                    v.wait_ge(_vE, M[("dtclip", icu, tsl)])
                    v.tensor_mul(adt_tm[:, tsl * 64:(tsl + 1) * 64],
                                 dt_tm[:, tsl * 64:(tsl + 1) * 64], AB).then_inc(_vE, 1)
                    M[("dtv", icu, tsl)] = vE.inc()

                block.vector(_dtv2)

                if t_dt is not None:
                    def _tapdt(s, tsl=tsl):
                        g2 = icu * 2 + tsl
                        s.wait_ge(_vE, M[("dtv", icu, tsl)])
                        gi = QS.grp(s)
                        s.dma_start(out=t_dt[g2 * 128:(g2 + 1) * 128, :],
                                    in_=dt_tm[:, tsl * 64:(tsl + 1) * 64]).then_inc(QS.hs[gi], 16)
                        M[("dtap", icu, tsl)] = QS.tick(gi)

                    block.sync(_tapdt)

        r2S = (r2, r2B)
        e_sbS = (e_sb, e_sbB)
        mt_sbS = (mt_sb, mt_sbB)
        hs_ssdS = (hs_ssd, hs_ssdB)
        bdecS = (bdec, bdecB)
        s16tS = (s16t, s16tB)
        yoS = (yo, yoB)
        y1S = (y1, y1B)

        def emit_ssd(icu):
            def _ac(t):
                t.wait_ge(_vE, M[("dtv", icu, 1)])
                t.matmul(pb[4][:, 0:64], t01f, adt_tm[:, 0:64], start=True, stop=True)
                t.matmul(pb[4][:, 64:128], onesf, adt_tm[:, 0:64], start=True, stop=False)
                t.matmul(pb[4][:, 64:128], t01f, adt_tm[:, 64:128], start=False, stop=True)
                t.matmul(pb[5][0:1, 0:64], onesf[:, 0:1], adt_tm[:, 0:64], start=True, stop=False)
                t.matmul(pb[5][0:1, 0:64], onesf[:, 0:1], adt_tm[:, 64:128],
                         start=False, stop=True).then_inc(_tE, 1)
                M[("acmm", icu)] = tE.inc()

            block.tensor(_ac)

            def _acv(v):
                v.wait_ge(_tE, M[("acmm", icu)])
                v.tensor_copy(acum, pb[4][:, 0:128])
                v.tensor_copy(trow[:, icu * 64:(icu + 1) * 64], pb[5][0:1, 0:64]).then_inc(_vE, 1)
                M[("acv", icu)] = vE.inc()

            block.vector(_acv)

            def _acT(t):
                t.wait_ge(_vE, M[("acv", icu)])
                t.transpose(pb[4][0:64, 0:128], acum[:, 0:64], idf)
                t.transpose(pb[4][0:64, 128:256], acum[:, 64:128], idf)
                t.matmul(pb[5][:, 0:64], onesf[0:1, :], trow[:, icu * 64:(icu + 1) * 64],
                         start=True, stop=True).then_inc(_tE, 1)
                M[("acT", icu)] = tE.inc()

            block.tensor(_acT)

            def _acTv(v):
                v.wait_ge(_tE, M[("acT", icu)])
                v.tensor_copy(acum_hm[:, 0:128], pb[4][0:64, 0:128])
                v.tensor_copy(acum_hm[:, 128:256], pb[4][0:64, 128:256])
                vss(v.tensor_copy(tcB, pb[5][:, 0:64]), v)
                for zt in range(2):
                    li = v.tensor_sub(dsub[:, zt * 64:(zt + 1) * 64], tcB,
                                      acum[:, zt * 64:(zt + 1) * 64])
                li.then_inc(_vE, 1)
                M[("acd", icu)] = vE.inc()

            block.vector(_acTv)

            def _aca(sc):
                sc.wait_ge(_vE, M[("acd", icu)])
                sc.activation(etcB, tcB, AF.Exp)
                sc.activation(decayb, dsub, AF.Exp)
                for zt in range(2):
                    gsl = icu * 2 + zt
                    li = sc.activation(sdo[:, gsl * 64:(gsl + 1) * 64],
                                       acum[:, zt * 64:(zt + 1) * 64], AF.Exp)
                li.then_inc(_aE, 1)
                M[("aca", icu)] = aE.inc()

            block.scalar(_aca)

            def _gm(t):
                t.wait_ge(_vE, M[("acd", icu)])
                t.wait_ge(_aE, M[("cvac", icu, 33)])
                t.wait_ge(_vE, M[("bTev", icu)])
                for lt in range(2):
                    li = t.matmul(pb[5][:, lt * 256:(lt + 1) * 256],
                                  bm_fm[:, lt * 128:(lt + 1) * 128],
                                  cm_fm[:, icu * 256:(icu + 1) * 256], start=True, stop=True)
                li.then_inc(_tE, 1)
                M[("gmmm", icu)] = tE.inc()

            block.tensor(_gm)

            def _gmv(v):
                v.wait_ge(_tE, M[("gmmm", icu)])
                for lt in range(2):
                    li = v.tensor_mul(gmm[:, lt * 256:(lt + 1) * 256],
                                      pb[5][:, lt * 256:(lt + 1) * 256],
                                      mask01[:, lt * 256:(lt + 1) * 256])
                li.then_inc(_vE, 1)
                M[("gmv", icu)] = vE.inc()

            block.vector(_gmv)

            for h in range(NH):
                sl = h % 2
                bk = pb[sl]

                def _pe1(t, h=h, sl=sl, bk=bk):
                    if h < 2:
                        t.wait_ge(_vE, M[("gmv", icu)])
                        t.wait_ge(_aE, M[("aca", icu)])
                        if icu > 0:
                            t.wait_ge(_aE, M[("yoh", icu - 1, NH - 2 + h)])
                    else:
                        t.wait_ge(_vE, M[("ssdv", icu, h - 2)])
                        t.wait_ge(_aE, M[("yoh", icu, h - 2)])
                    t.matmul(bk[:, 0:256], idf[0:64, h:h + 1].broadcast_to([64, 128]),
                             acum_hm[0:64, 0:256], start=True, stop=True).then_inc(_tE, 1)
                    M[("rowA", icu, h)] = tE.inc()

                block.tensor(_pe1)

                def _dv1(v, h=h, sl=sl, bk=bk):
                    v.wait_ge(_tE, M[("rowA", icu, h)])
                    for lt in range(2):
                        li = v.tensor_scalar(r2S[sl][:, lt * 256:(lt + 1) * 256], bk[:, 0:256],
                                             acum[:, lt * 64 + h:lt * 64 + h + 1], 0.0,
                                             ALU.subtract, ALU.min)
                    li.then_inc(_vE, 1)
                    M[("r2", icu, h)] = vE.inc()

                block.vector(_dv1)

                def _sc1(sc, h=h, sl=sl):
                    sc.wait_ge(_vE, M[("r2", icu, h)])
                    sc.activation(e_sbS[sl][:, 0:512], r2S[sl][:, 0:512],
                                  AF.Exp).then_inc(_aE, 1)
                    M[("eh", icu, h)] = aE.inc()

                block.scalar(_sc1)

                def _dv2(v, h=h, sl=sl):
                    v.wait_ge(_aE, M[("eh", icu, h)])
                    if h >= 2:
                        v.wait_ge(_tE, M[("mmh", icu, h - 2)])
                    else:
                        v.wait_ge(_tE, M[("gmmm", icu)])
                    for lt in range(2):
                        v.tensor_mul(mt_sbS[sl][:, lt * 256:(lt + 1) * 256],
                                     gmm[:, lt * 256:(lt + 1) * 256],
                                     e_sbS[sl][:, lt * 256:(lt + 1) * 256])
                        v.tensor_scalar(hs_ssdS[sl][:, lt * 64:(lt + 1) * 64],
                                        hs_tm[:, lt * INTER + h * 64: lt * INTER + h * 64 + 64],
                                        dt_tm[:, lt * 64 + h:lt * 64 + h + 1], None, ALU.mult)
                        v.tensor_scalar(bdecS[sl][:, lt * 128:(lt + 1) * 128],
                                        bm_tm[:, lt * 128:(lt + 1) * 128],
                                        decayb[:, lt * 64 + h:lt * 64 + h + 1], None, ALU.mult)
                    v.tensor_copy(s16tS[sl], S[:, h * HP:(h + 1) * HP]).then_inc(_vE, 1)
                    M[("mkv", icu, h)] = vE.inc()

                block.vector(_dv2)

                def _pe2(t, h=h, sl=sl, bk=bk):
                    t.wait_ge(_vE, M[("mkv", icu, h)])
                    for zt in range(2):
                        for lt in range(2):
                            t.matmul(bk[:, 256 + zt * 64:256 + (zt + 1) * 64],
                                     mt_sbS[sl][:, lt * 256 + zt * 128: lt * 256 + zt * 128 + 128],
                                     hs_ssdS[sl][:, lt * 64:(lt + 1) * 64],
                                     start=(lt == 0), stop=(lt == 1))
                    for lt in range(2):
                        t.matmul(bk[:, 384:448], bdecS[sl][:, lt * 128:(lt + 1) * 128],
                                 hs_ssdS[sl][:, lt * 64:(lt + 1) * 64],
                                 start=(lt == 0), stop=(lt == 1))
                    for zt in range(2):
                        li = t.matmul(bk[:, zt * 64:(zt + 1) * 64],
                                      cm_fm[:, icu * 256 + zt * 128: icu * 256 + zt * 128 + 128],
                                      s16tS[sl], start=True, stop=True)
                    li.then_inc(_tE, 1)
                    M[("mmh", icu, h)] = tE.inc()

                block.tensor(_pe2)

                def _sc2(sc, h=h, sl=sl, bk=bk):
                    sc.wait_ge(_tE, M[("mmh", icu, h)])
                    if h >= 2:
                        sc.wait_ge(_vE, M[("ssdv", icu, h - 2)])
                    elif icu > 0:
                        sc.wait_ge(_vE, M[("ssdv", icu - 1, NH - 2 + h)])
                    for zt in range(2):
                        gsl = icu * 2 + zt
                        li = sc.activation(yoS[sl][:, zt * 64:(zt + 1) * 64],
                                           bk[:, zt * 64:(zt + 1) * 64],
                                           AF.Copy, scale=sdo[:, gsl * 64 + h:gsl * 64 + h + 1])
                    li.then_inc(_aE, 1)
                    M[("yoh", icu, h)] = aE.inc()

                block.scalar(_sc2)

                def _dv3(v, h=h, sl=sl, bk=bk):
                    v.wait_ge(_aE, M[("yoh", icu, h)])
                    v.tensor_add(y1S[sl][:, 0:64], bk[:, 256:320], yoS[sl][:, 0:64])
                    vss(v.tensor_add(y1S[sl][:, 64:128], bk[:, 320:384],
                                     yoS[sl][:, 64:128]), v)
                    for zt in range(2):
                        gsl = icu * 2 + zt
                        v.scalar_tensor_tensor(yb[:, gsl * INTER + h * 64: gsl * INTER + h * 64 + 64],
                                               hs_tm[:, zt * INTER + h * 64: zt * INTER + h * 64 + 64],
                                               DB[:, h:h + 1], y1S[sl][:, zt * 64:(zt + 1) * 64],
                                               ALU.mult, ALU.add)
                    v.scalar_tensor_tensor(S[:, h * HP:(h + 1) * HP], S[:, h * HP:(h + 1) * HP],
                                           etcB[:, h:h + 1], bk[:, 384:448],
                                           ALU.mult, ALU.add).then_inc(_vE, 1)
                    M[("ssdv", icu, h)] = vE.inc()

                block.vector(_dv3)

        for icu in range(NCH):
            for tsl in range(2):
                g = icu * 2 + tsl
                extra = [(_vE, M["bc_ca"]), (_vE, M["bc_sa"])] if g == 0 else []
                pef = M[("ssdfree", icu - 1)] if ("ssdfree", icu - 1) in M else None
                emit_hslice("hA", g, h_fm, 256, x_d[g * 128:(g + 1) * 128, :],
                            extra_v_waits=extra, pe_free_mark=None)
            M[("hfm_done", icu)] = vE.n
            emit_chunk_inproj(icu)
            emit_ssd(icu)
            M[("ssdfree", icu)] = vE.n
            if t_hs is not None:
                def _taphs(gx, icu=icu):
                    gx.wait_ge(_vE, M[("hsTev", icu, 31)])
                    gi = QG.grp(gx)
                    for sl in range(2):
                        g2 = icu * 2 + sl
                        gx.dma_start(out=t_hs[g2 * 128:(g2 + 1) * 128, :],
                                     in_=hs_tm[:, sl * INTER:(sl + 1) * INTER]).then_inc(QG.hs[gi], 16)
                        QG.tick(gi)
                    M[("taphs", icu)] = QG.mark(gi)

                block.gpsimd(_taphs)

                def _taphsw(v, icu=icu):
                    wq(v, M[("taphs", icu)])

                block.vector(_taphsw)
            if t_bc is not None:
                def _tapbc(g2, icu=icu):
                    g2.wait_ge(_vE, M[("bTev", icu)])
                    g2.wait_ge(_aE, M[("cvac", icu, 33)])
                    gi = QG.grp(g2)
                    g2.dma_start(out=t_bc[0:128, icu * 256:(icu + 1) * 256],
                                 in_=bm_fm[:, 0:256]).then_inc(QG.hs[gi], 16)
                    QG.tick(gi)
                    g2.dma_start(out=t_bc[128:256, icu * 256:(icu + 1) * 256],
                                 in_=cm_fm[:, icu * 256:(icu + 1) * 256]).then_inc(QG.hs[gi], 16)
                    M[("tapbc", icu)] = QG.tick(gi)

                block.gpsimd(_tapbc)

                def _tapbcw(v, icu=icu):
                    wq(v, M[("tapbc", icu)])

                block.vector(_tapbcw)


        if upto == "A":
            emit_stub_fin()
            return nc, tap_d

        def pB_v(v):
            v.wait_ge(_vE, M[("ssdv", NCH - 1, NH - 1)])
            wq(v, M["smalls"])
            v.tensor_scalar(sstage, S, msend_sb, None, ALU.mult).then_inc(_vE, 1)
            M["sstage"] = vE.inc()

        block.vector(pB_v)

        def pB_g(g):
            g.wait_ge(_vE, M["sstage"])
            gi = QG.grp(g)
            g.dma_start(out=ccin_st, in_=sstage).then_inc(QG.hs[gi], 16)
            wq(g, QG.tick(gi))
            g.collective_compute("AllReduce", ALU.add,
                                 replica_groups=[[0, 1], [2, 3], [4, 5], [6, 7]],
                                 ins=[ccin_st], outs=[ccout_st]).then_inc(_cc, 1)
            cc.inc()
            g.wait_ge(_cc, cc.n)
            gi = QG.grp(g)
            g.dma_start(out=sstage, in_=ccout_st).then_inc(QG.hs[gi], 16)
            M["srecv"] = QG.tick(gi)

        block.gpsimd(pB_g)

        def pB_v2(v):
            wq(v, M["srecv"])
            v.tensor_scalar(S, sstage, mrecv_sb, None, ALU.mult).then_inc(_vE, 1)
            M["Sin"] = vE.inc()

        block.vector(pB_v2)

        for c in range(NCH):
            if c > 0:
                def _ev(v, c=c):
                    if c > 1:
                        v.wait_ge(_vE, M[("evr", c - 1)])
                    v.tensor_add(evrow[0:1, 0:64], evrow[0:1, 0:64],
                                 trow[:, (c - 1) * 64:c * 64]).then_inc(_vE, 1)
                    M[("evr", c)] = vE.inc()

                def _ea(sc, c=c):
                    sc.wait_ge(_vE, M[("evr", c)])
                    sc.activation(evrow[0:1, 64:128], evrow[0:1, 0:64], AF.Exp).then_inc(_aE, 1)
                    M[("eva", c)] = aE.inc()

                def _et(t, c=c):
                    t.wait_ge(_aE, M[("eva", c)])
                    t.matmul(pb[4][:, 0:64], onesf[0:1, :], evrow[0:1, 64:128],
                             start=True, stop=True).then_inc(_tE, 1)
                    M[("evt", c)] = tE.inc()

                def _ev2(v, c=c):
                    v.wait_ge(_tE, M[("evt", c)])
                    v.tensor_copy(tcB, pb[4][:, 0:64]).then_inc(_vE, 1)
                    M[("evo", c)] = vE.inc()

                block.vector(_ev)
                block.scalar(_ea)
                block.tensor(_et)
                block.vector(_ev2)

            for h in range(NH):
                sl = h % 2
                bkB = pb[sl]

                def _cs(v, c=c, h=h, sl=sl):
                    if c > 0 and h < 2:
                        v.wait_ge(_vE, M[("evo", c)])
                    if h >= 2:
                        v.wait_ge(_tE, M[("cmm", c, h - 2)])
                    elif c > 0:
                        v.wait_ge(_tE, M[("cmm", c - 1, NH - 2 + h)])
                    if c == 0 and h == 0:
                        v.wait_ge(_vE, M["Sin"])
                    if c == 0:
                        v.tensor_copy(s16tS[sl], S[:, h * HP:(h + 1) * HP]).then_inc(_vE, 1)
                    else:
                        v.tensor_scalar(s16tS[sl], S[:, h * HP:(h + 1) * HP], tcB[:, h:h + 1],
                                        None, ALU.mult).then_inc(_vE, 1)
                    M[("cs16", c, h)] = vE.inc()

                def _cm(t, c=c, h=h, sl=sl, bkB=bkB):
                    t.wait_ge(_vE, M[("cs16", c, h)])
                    if h >= 2:
                        t.wait_ge(_aE, M[("cyo", c, h - 2)])
                    elif c > 0:
                        t.wait_ge(_aE, M[("cyo", c - 1, NH - 2 + h)])
                    else:
                        t.wait_ge(_aE, M[("yoh", NCH - 1, NH - 1)])
                    for zt in range(2):
                        li = t.matmul(bkB[:, zt * 64:(zt + 1) * 64],
                                      cm_fm[:, c * 256 + zt * 128: c * 256 + zt * 128 + 128],
                                      s16tS[sl], start=True, stop=True)
                    li.then_inc(_tE, 1)
                    M[("cmm", c, h)] = tE.inc()

                def _cy(sc, c=c, h=h, sl=sl, bkB=bkB):
                    sc.wait_ge(_tE, M[("cmm", c, h)])
                    if h >= 2:
                        sc.wait_ge(_vE, M[("corrv", c, h - 2)])
                    elif c > 0:
                        sc.wait_ge(_vE, M[("corrv", c - 1, NH - 2 + h)])
                    for zt in range(2):
                        gsl = c * 2 + zt
                        li = sc.activation(yoS[sl][:, zt * 64:(zt + 1) * 64],
                                           bkB[:, zt * 64:(zt + 1) * 64],
                                           AF.Copy, scale=sdo[:, gsl * 64 + h:gsl * 64 + h + 1])
                    li.then_inc(_aE, 1)
                    M[("cyo", c, h)] = aE.inc()

                def _cv(v, c=c, h=h, sl=sl):
                    v.wait_ge(_aE, M[("cyo", c, h)])
                    for zt in range(2):
                        gsl = c * 2 + zt
                        slc = yb[:, gsl * INTER + h * 64: gsl * INTER + h * 64 + 64]
                        li = v.tensor_add(slc, slc, yoS[sl][:, zt * 64:(zt + 1) * 64])
                    li.then_inc(_vE, 1)
                    M[("corrv", c, h)] = vE.inc()

                block.vector(_cs)
                block.tensor(_cm)
                block.scalar(_cy)
                block.vector(_cv)

        M["ycorr"] = vE.n

        if t_y is not None:
            def _tapy(g, _ty=t_y):
                g.wait_ge(_vE, M["ycorr"])
                gi = QG.grp(g)
                for g2 in range(NTS):
                    g.dma_start(out=_ty[g2 * 128:(g2 + 1) * 128, :],
                                in_=yb[:, g2 * INTER:(g2 + 1) * INTER]).then_inc(QG.hs[gi], 16)
                    QG.tick(gi)
                M["tapy"] = QG.mark(gi)

            block.gpsimd(_tapy)

            def _tapyw(v):
                wq(v, M["tapy"])

            block.vector(_tapyw)

        if upto == "B":
            emit_stub_fin()
            return nc, tap_d

        def emit_bcast_dram(tag, dst, src_dram_col, ncols, add1=False, bank=6,
                            pre_t=(), pre_s=()):
            nj = (ncols + 511) // 512
            prev_ev = vE.n
            for j in range(nj):
                def _sy(s, j=j):
                    if j == 0:
                        for wk in pre_s:
                            s.wait_ge(wk[0], wk[1])
                    if j > 0:
                        s.wait_ge(_tE, M[(tag, "mm", j - 1)])
                    elif "rowst_free" in M:
                        s.wait_ge(_tE, M["rowst_free"])
                    gi = QS.grp(s)
                    s.dma_start(out=rowst, in_=src_dram_col[:, j * 512:(j + 1) * 512]).then_inc(QS.hs[gi], 16)
                    M[(tag, "ld", j)] = QS.tick(gi)

                def _t(t, j=j):
                    wq(t, M[(tag, "ld", j)])
                    if j == 0:
                        t.wait_ge(_vE, prev_ev)
                        for sm, val in pre_t:
                            t.wait_ge(sm, val)
                    if j >= 1:
                        t.wait_ge(_vE, M[(tag, "ev", j - 1)])
                    t.matmul(pb[bank][:, 0:512], onesf[0:1, :], rowst,
                             start=True, stop=True).then_inc(_tE, 1)
                    M[(tag, "mm", j)] = tE.inc()

                def _v(v, j=j):
                    v.wait_ge(_tE, M[(tag, "mm", j)])
                    if add1:
                        v.tensor_scalar(dst[:, j * 512:(j + 1) * 512], pb[bank][:, 0:512],
                                        1.0, None, ALU.add).then_inc(_vE, 1)
                    else:
                        v.tensor_copy(dst[:, j * 512:(j + 1) * 512], pb[bank][:, 0:512]).then_inc(_vE, 1)
                    M[(tag, "ev", j)] = vE.inc()

                block.sync(_sy)
                block.tensor(_t)
                block.vector(_v)
            M[tag] = vE.n
            M["rowst_free"] = M[(tag, "mm", nj - 1)]

        emit_bcast_dram("bc_ga", bcP2, mod_dram[:, 2 * HID:3 * HID], HID,
                        pre_s=[M["mod_dram"]])
        emit_bcast_dram("bc_nw", normwB, normw_d, INTER)

        for g in range(NTS):
            emit_hslice("hC", g, hfmC, T, x_d[g * 128:(g + 1) * 128, :])
        M["hC_done"] = vE.n

        if t_hc is not None:
            def _taphc(g2):
                g2.wait_ge(_vE, M["hC_done"])
                gi = QG.grp(g2)
                g2.dma_start(out=t_hc, in_=hfmC).then_inc(QG.hs[gi], 16)
                M["taphc"] = QG.tick(gi)

            block.gpsimd(_taphc)

            def _taphcw(t):
                wq(t, M["taphc"])

            block.tensor(_taphcw)

        for mg in range(16):
            buf = mg % 2
            tagw = ("wtgc", mg)

            def _stage(s, mg=mg, buf=buf, tagw=tagw):
                if mg >= 2:
                    s.wait_ge(_tE, M[("gmm", mg - 2, NTS - 1)])
                else:
                    s.wait_ge(_vE, M["ycorr"])
                r0 = mg * 256
                gi = QG.grp(s)
                for kq in range(4):
                    s.dma_start(out=wtgC[:, buf * 4096 + kq * 1024:
                                         buf * 4096 + (kq + 1) * 1024].rearrange(
                                    "p (k f) -> p k f", k=4),
                                in_=w_inT3[:, kq * 4:(kq + 1) * 4,
                                           r0:r0 + 256]).then_inc(QG.hs[gi], 16)
                    QG.tick(gi)
                M[tagw] = QG.mark(gi)

            block.gpsimd(_stage)

            for g in range(NTS):
                def _mm(t, g=g, mg=mg, buf=buf, tagw=tagw):
                    if g == 0:
                        wq(t, M[tagw])
                    if mg == 0 and g == 0:
                        t.wait_ge(_vE, M["hC_done"])
                    bk = pb[g % 2]
                    key = ("gac", mg, g - 2) if g >= 2 else (("gac", mg - 1, NTS - 2 + g) if mg > 0 else None)
                    if key:
                        t.wait_ge(_aE, M[key])
                    for k in range(KT):
                        li = t.matmul(bk[:, 0:256], hfmC[:, k * T + g * 128: k * T + g * 128 + 128],
                                      wtgC[:, buf * 4096 + k * 256:buf * 4096 + (k + 1) * 256],
                                      start=(k == 0), stop=(k == KT - 1))
                    li.then_inc(_tE, 1)
                    M[("gmm", mg, g)] = tE.inc()

                def _ac(sc, g=g, mg=mg):
                    bk = pb[g % 2]
                    sc.wait_ge(_tE, M[("gmm", mg, g)])
                    key = ("ygv", mg, g - 1) if g >= 1 else (("ygv", mg - 1, NTS - 1) if mg > 0 else None)
                    if key:
                        sc.wait_ge(_vE, M[key])
                    sc.activation(e_sb[:, 0:256], bk[:, 0:256], AF.Silu).then_inc(_aE, 1)
                    M[("gac", mg, g)] = aE.inc()

                def _yg(v, g=g, mg=mg):
                    v.wait_ge(_aE, M[("gac", mg, g)])
                    if mg == 0 and g == 0 and "tapy" in M:
                        wq(v, M["tapy"])
                    sl = yb[:, g * INTER + mg * 256: g * INTER + mg * 256 + 256]
                    vss(v.tensor_mul(sl, sl, e_sb[:, 0:256]), v)
                    v.scalar_tensor_tensor(scr2[:, 0:256], sl, 1.0, sl, ALU.mult, ALU.mult,
                                           accum_out=ssqp[:, g * 16 + mg:g * 16 + mg + 1]).then_inc(_vE, 1)
                    M[("ygv", mg, g)] = vE.inc()

                block.tensor(_mm)
                block.scalar(_ac)
                block.vector(_yg)

        M["yg_done"] = vE.n

        def _rms_v(v):
            v.wait_ge(_vE, M["yg_done"])
            for g in range(NTS):
                li = v.reduce_sum(st8[:, 8 + g:9 + g], ssqp[:, g * 16:(g + 1) * 16], axis=AX.X)
            li.then_inc(_vE, 1)
            M["ssq"] = vE.inc()

        def _rms_a(sc):
            sc.wait_ge(_vE, M["ssq"])
            ass(sc.activation(st8[:, 8:16], st8[:, 8:16], AF.Ln, scale=cst[:, 5:6],
                              bias=cst[:, 4:5]), sc)
            sc.activation(st8[:, 8:16], st8[:, 8:16], AF.Exp, scale=cst[:, 3:4]).then_inc(_aE, 1)
            M["rstd"] = aE.inc()

        block.vector(_rms_v)
        block.scalar(_rms_a)

        for g in range(NTS):
            def _yn(v, g=g):
                if g == 0:
                    v.wait_ge(_aE, M["rstd"])
                    v.wait_ge(_vE, M["bc_nw"])
                v.scalar_tensor_tensor(yb[:, g * INTER:(g + 1) * INTER],
                                       yb[:, g * INTER:(g + 1) * INTER],
                                       st8[:, 8 + g:9 + g], normwB, ALU.mult,
                                       ALU.mult).then_inc(_vE, 1)
                M[("ynv", g)] = vE.inc()

            block.vector(_yn)

        if t_yn is not None:
            def _tapyn(g):
                g.wait_ge(_vE, M[("ynv", NTS - 1)])
                gi = QG.grp(g)
                for g2 in range(NTS):
                    g.dma_start(out=t_yn[g2 * 128:(g2 + 1) * 128, :],
                                in_=yb[:, g2 * INTER:(g2 + 1) * INTER]).then_inc(QG.hs[gi], 16)
                    QG.tick(gi)
                M["tapyn"] = QG.mark(gi)

            block.gpsimd(_tapyn)

        for g in range(NTS):
            for q in range(KI // 4):
                def _t(t, g=g, q=q):
                    if g == 0 and q == 0:
                        t.wait_ge(_vE, M[("ynv", 0)])
                        if "tapyn" in M:
                            wq(t, M["tapyn"])
                    elif q == 0:
                        t.wait_ge(_vE, M[("ynv", g)])
                    if ("ynT_ev", g, q - 1) in M:
                        t.wait_ge(_vE, M[("ynT_ev", g, q - 1)])
                    elif g > 0 and ("ynT_ev", g - 1, KI // 4 - 1) in M:
                        t.wait_ge(_vE, M[("ynT_ev", g - 1, KI // 4 - 1)])
                    for kk in range(4):
                        kb = q * 4 + kk
                        li = t.transpose(pt0[:, kk * 128:(kk + 1) * 128],
                                         yb[:, g * INTER + kb * 128: g * INTER + kb * 128 + 128],
                                         iden)
                    li.then_inc(_tE, 1)
                    M[("ynT", g, q)] = tE.inc()

                def _v(v, g=g, q=q):
                    v.wait_ge(_tE, M[("ynT", g, q)])
                    for kk in range(4):
                        kb = q * 4 + kk
                        li = v.tensor_copy(ynf[:, kb * T + g * 128: kb * T + g * 128 + 128],
                                           pt0[:, kk * 128:(kk + 1) * 128])
                    li.then_inc(_vE, 1)
                    M[("ynT_ev", g, q)] = vE.inc()

                block.tensor(_t)
                block.vector(_v)
        M["ynf_done"] = vE.n

        for dg in range(8):
            buf = dg % 2
            tagw = ("wotg", dg)

            def _stage(s, dg=dg, buf=buf, tagw=tagw):
                if dg == 0:
                    s.wait_ge(_vE, M["ynf_done"])
                if dg >= 2:
                    s.wait_ge(_tE, M[("opmm", dg - 2, NTS - 1)])
                r0 = dg * 256
                gi = QG.grp(s)
                for iq in range(8):
                    s.dma_start(out=wot[:, buf * 8192 + iq * 1024:
                                        buf * 8192 + (iq + 1) * 1024].rearrange(
                                    "p (i f) -> p i f", i=4),
                                in_=w_outT3[:, iq * 4:(iq + 1) * 4,
                                            r0:r0 + 256]).then_inc(QG.hs[gi], 16)
                    QG.tick(gi)
                M[tagw] = QG.mark(gi)

            block.gpsimd(_stage)

            for g in range(NTS):
                def _xl(s, dg=dg, g=g):
                    slot = (dg * NTS + g) % 2
                    key = ("opv", dg, g - 2) if g >= 2 else (("opv", dg - 1, NTS - 2 + g) if dg > 0 else None)
                    if key and key in M:
                        s.wait_ge(_vE, M[key])
                    elif dg == 0:
                        s.wait_ge(_vE, M["hC_done"])
                    gi = QS.grp(s)
                    s.dma_start(out=xt[:, slot * 256:(slot + 1) * 256],
                                in_=x_d[g * 128:(g + 1) * 128, dg * 256:(dg + 1) * 256]).then_inc(QS.hs[gi], 16)
                    M[("opx", dg, g)] = QS.tick(gi)

                def _mm(t, dg=dg, g=g, buf=buf, tagw=tagw):
                    if g == 0:
                        wq(t, M[tagw])
                    bk = pb[g % 2]
                    key = ("opv", dg, g - 2) if g >= 2 else (("opv", dg - 1, NTS - 2 + g) if dg > 0 else None)
                    if key and key in M:
                        t.wait_ge(_vE, M[key])
                    for i in range(KI):
                        li = t.matmul(bk[:, 0:256], ynf[:, i * T + g * 128: i * T + g * 128 + 128],
                                      wot[:, buf * 8192 + i * 256:buf * 8192 + (i + 1) * 256],
                                      start=(i == 0), stop=(i == KI - 1))
                    li.then_inc(_tE, 1)
                    M[("opmm", dg, g)] = tE.inc()

                def _ev(v, dg=dg, g=g):
                    slot = (dg * NTS + g) % 2
                    v.wait_ge(_tE, M[("opmm", dg, g)])
                    wq(v, M[("opx", dg, g)])
                    idx = dg * NTS + g
                    pk = (idx - 2) // NTS, (idx - 2) % NTS
                    if idx >= 2 and ("opst", pk[0], pk[1]) in M:
                        wq(v, M[("opst", pk[0], pk[1])])
                    if dg == 0 and g == 0:
                        v.wait_ge(_vE, M["bc_ga"])
                    vss(v.tensor_mul(scr2[:, 256:512], pb[g % 2][:, 0:256],
                                     bcP2[:, dg * 256:(dg + 1) * 256]), v)
                    v.tensor_add(scr2[:, 512 + slot * 256:512 + (slot + 1) * 256],
                                 scr2[:, 256:512], xt[:, slot * 256:(slot + 1) * 256]).then_inc(_vE, 1)
                    M[("opv", dg, g)] = vE.inc()

                def _st(s, dg=dg, g=g):
                    slot = (dg * NTS + g) % 2
                    s.wait_ge(_vE, M[("opv", dg, g)])
                    gi = QS.grp(s)
                    s.dma_start(out=xnew_dram[g * 128:(g + 1) * 128, dg * 256:(dg + 1) * 256],
                                in_=scr2[:, 512 + slot * 256:512 + (slot + 1) * 256]).then_inc(QS.hs[gi], 16)
                    M[("opst", dg, g)] = QS.tick(gi)

                block.sync(_xl)
                block.tensor(_mm)
                block.vector(_ev)
                block.sync(_st)


        if upto == "C":
            def _fin(s):
                s.wait_ge(_vE, vE.n)
                s.wait_ge(_aE, aE.n)
                s.wait_ge(_tE, tE.n)
                for mk in QS.finals() + QG.finals():
                    wq(s, mk)
                gi = QS.grp(s)
                for g2 in range(NTS):
                    s.dma_start(out=out_d[g2 * 128:(g2 + 1) * 128, :],
                                in_=xnew_dram[g2 * 128:(g2 + 1) * 128, :]).then_inc(QS.hs[gi], 16)
                    QS.tick(gi)
                wq(s, QS.mark(gi))

            block.sync(_fin)
            return nc, tap_d

        emit_bcast_dram("bc_cb", bcP0, mod_dram[:, 4 * HID:5 * HID], HID, add1=True,
                        pre_t=[(_vE, M["hC_done"])])
        emit_bcast_dram("bc_sb", bcP1, mod_dram[:, 3 * HID:4 * HID], HID)
        emit_bcast_dram("bc_gb", bcP2, mod_dram[:, 5 * HID:6 * HID], HID,
                        pre_t=[(_vE, M[("opv", 7, NTS - 1)])])

        for g in range(NTS):
            def _w(v, g=g):
                if g == 0:
                    v.wait_ge(_vE, M["bc_cb"])
                    v.wait_ge(_vE, M["bc_sb"])

            block.vector(_w)
            emit_hslice("hD", g, h2f, T, xnew_dram[g * 128:(g + 1) * 128, :],
                        extra_s_waits=[M[("opst", dgx, g)] for dgx in range(8)])
        M["h2_done"] = vE.n

        for g in range(NTS):
            def _rt(t, g=g):
                if g == 0:
                    t.wait_ge(_vE, M["h2_done"])
                    wq(t, M["smalls"])
                if ("rtv", g - 1) in M:
                    t.wait_ge(_vE, M[("rtv", g - 1)])
                    t.wait_ge(_aE, M[("sig", g - 1)])
                for k in range(KT):
                    t.matmul(pb[4][:, 0:NEXP], h2f[:, k * T + g * 128: k * T + g * 128 + 128],
                             wr_sb[:, k * NEXP:(k + 1) * NEXP], start=(k == 0), stop=(k == KT - 1))
                for k in range(KT):
                    li = t.matmul(pb[4][:, 8:9], h2f[:, k * T + g * 128: k * T + g * 128 + 128],
                                  wsg_sb[:, k:k + 1], start=(k == 0), stop=(k == KT - 1))
                li.then_inc(_tE, 1)
                M[("rt", g)] = tE.inc()

            block.tensor(_rt)

            def _sm1(v, g=g):
                v.wait_ge(_tE, M[("rt", g)])
                v.reduce_max(st8[:, 0:1], pb[4][:, 0:NEXP], axis=AX.X).then_inc(_vE, 1)
                M[("sm1", g)] = vE.inc()

            def _sm2(sc, g=g):
                sc.wait_ge(_vE, M[("sm1", g)])
                ass(sc.activation(st8[:, 1:2], st8[:, 0:1], AF.Copy, scale=-1.0), sc)
                sc.activation(scr2[:, 0:8], pb[4][:, 0:NEXP], AF.Exp, bias=st8[:, 1:2])
                sc.activation(sig_sb[:, g:g + 1], pb[4][:, 8:9], AF.Sigmoid).then_inc(_aE, 1)
                M[("sig", g)] = aE.inc()

            def _sm3(v, g=g):
                v.wait_ge(_aE, M[("sig", g)])
                v.reduce_sum(st8[:, 2:3], scr2[:, 0:8], axis=AX.X).then_inc(_vE, 1)
                M[("smA", g)] = vE.inc()
                v.wait_ge(_vE, M[("smA", g)])
                v.reciprocal(st8[:, 3:4], st8[:, 2:3]).then_inc(_vE, 1)
                M[("smB", g)] = vE.inc()
                v.wait_ge(_vE, M[("smB", g)])
                v.tensor_scalar(scr2[:, 8:16], scr2[:, 0:8], st8[:, 3:4], None, ALU.mult).then_inc(_vE, 1)
                M[("smC", g)] = vE.inc()
                v.wait_ge(_vE, M[("smC", g)])
                v.reduce_max(st8[:, 4:5], scr2[:, 8:16], axis=AX.X).then_inc(_vE, 1)
                M[("smD", g)] = vE.inc()
                v.wait_ge(_vE, M[("smD", g)])
                v.tensor_scalar(scr2[:, 16:24], scr2[:, 8:16], st8[:, 4:5], None, ALU.is_ge).then_inc(_vE, 1)
                M[("smE", g)] = vE.inc()
                v.wait_ge(_vE, M[("smE", g)])
                v.scalar_tensor_tensor(scr2[:, 24:32], scr2[:, 16:24], -1e30, scr2[:, 8:16],
                                       ALU.mult, ALU.add).then_inc(_vE, 1)
                M[("smF", g)] = vE.inc()
                v.wait_ge(_vE, M[("smF", g)])
                v.reduce_max(st8[:, 5:6], scr2[:, 24:32], axis=AX.X).then_inc(_vE, 1)
                M[("smG", g)] = vE.inc()
                v.wait_ge(_vE, M[("smG", g)])
                v.tensor_scalar(scr2[:, 16:24], scr2[:, 8:16], st8[:, 5:6], None, ALU.is_ge).then_inc(_vE, 1)
                M[("smH", g)] = vE.inc()
                v.wait_ge(_vE, M[("smH", g)])
                v.tensor_mul(c_sb[:, g * 8:(g + 1) * 8], scr2[:, 8:16], scr2[:, 16:24]).then_inc(_vE, 1)
                M[("rtv", g)] = vE.inc()

            block.vector(_sm1)
            block.scalar(_sm2)
            block.vector(_sm3)

        if t_c is not None:
            def _tapc(s):
                s.wait_ge(_vE, M[("rtv", NTS - 1)])
                gi = QS.grp(s)
                for g2 in range(NTS):
                    s.dma_start(out=t_c[g2 * 128:(g2 + 1) * 128, :],
                                in_=c_sb[:, g2 * 8:(g2 + 1) * 8]).then_inc(QS.hs[gi], 16)
                    QS.tick(gi)

            block.sync(_tapc)

        blocks_ffn = [("E", e) for e in range(NEXP)] + [("S", gr) for gr in range(4)]
        NB = len(blocks_ffn)
        NFT = 11
        sd3 = sd_d.rearrange("(f p) d -> p f d", p=128)
        wd3 = [wd_d[e].rearrange("(f p) d -> p f d", p=128) for e in range(NEXP)]

        def gu_src(bi, ft):
            kind, idx = blocks_ffn[bi]
            if kind == "E":
                return wgT_d[idx][ft], wuT_d[idx][ft]
            return sgT_d[idx * NFT + ft], suT_d[idx * NFT + ft]

        def wd_src(bi):
            kind, idx = blocks_ffn[bi]
            if kind == "E":
                return wd3[idx], 0
            return sd3, idx * NFT

        def scale_col(bi, g):
            kind, idx = blocks_ffn[bi]
            if kind == "E":
                return c_sb[:, g * 8 + idx:g * 8 + idx + 1]
            return sig_sb[:, g:g + 1]

        def emit_gul(s):
            bi, ft = divmod(s, NFT)
            buf = s % 2
            wsrc, usrc = gu_src(bi, ft)

            def _ld(g2, s=s, buf=buf, wsrc=wsrc, usrc=usrc):
                if ("gumm_last", s - 2) in M:
                    g2.wait_ge(_tE, M[("gumm_last", s - 2)])
                else:
                    g2.wait_ge(_vE, M[("ynv", NTS - 1)])
                gi = QG.grp(g2)
                g2.dma_start(out=wgt2[:, buf * 2048:(buf + 1) * 2048],
                             in_=wsrc).then_inc(QG.hs[gi], 16)
                QG.tick(gi)
                g2.dma_start(out=wut2[:, buf * 2048:(buf + 1) * 2048],
                             in_=usrc).then_inc(QG.hs[gi], 16)
                M[("gul", s)] = QG.tick(gi)

            block.gpsimd(_ld)

        def emit_wdl(bi, dq):
            wsrc, fof = wd_src(bi)
            dst = wdq0 if dq % 2 == 0 else wdq1

            def _ld(sy, bi=bi, dq=dq, wsrc=wsrc, fof=fof, dst=dst):
                if dq >= 2:
                    sy.wait_ge(_tE, M[("dmm_last", bi, dq - 2)])
                elif bi > 0:
                    sy.wait_ge(_tE, M[("dmm_last", bi - 1, dq + 2)])
                elif dq == 0:
                    sy.wait_ge(_vE, M["ycorr"])
                else:
                    sy.wait_ge(_vE, M["h2_done"])
                gi = QG.grp(sy)
                for f0, f1 in ((0, 4), (4, 8), (8, 11)):
                    sy.dma_start(out=dst[:, f0 * 512:f1 * 512].rearrange(
                                     "p (f d) -> p f d", f=f1 - f0),
                                 in_=wsrc[:, fof + f0:fof + f1,
                                          dq * 512:(dq + 1) * 512]).then_inc(QG.hs[gi], 16)
                    QG.tick(gi)
                M[("wdl", bi, dq)] = QG.mark(gi)

            block.gpsimd(_ld)

        def emit_gu_step(s):
            bi, ft = divmod(s, NFT)
            buf = s % 2
            for tch in range(2):
                bkG = pb[0 if tch == 0 else 2]
                bkU = pb[1 if tch == 0 else 3]

                def _mm(t, s=s, ft=ft, buf=buf, tch=tch, bkG=bkG, bkU=bkU):
                    if tch == 0:
                        wq(t, M[("gul", s)])
                    if ("gua", s - 1, tch) in M:
                        t.wait_ge(_aE, M[("gua", s - 1, tch)])
                        t.wait_ge(_vE, M[("guv", s - 1, tch)])
                    elif s == 0 and tch == 0:
                        t.wait_ge(_vE, M["h2_done"])
                    for k in range(KT):
                        t.matmul(bkG[:, 0:512],
                                 wgt2[:, buf * 2048 + k * 128: buf * 2048 + (k + 1) * 128],
                                 h2f[:, k * T + tch * 512: k * T + (tch + 1) * 512],
                                 start=(k == 0), stop=(k == KT - 1))
                    for k in range(KT):
                        li = t.matmul(bkU[:, 0:512],
                                      wut2[:, buf * 2048 + k * 128: buf * 2048 + (k + 1) * 128],
                                      h2f[:, k * T + tch * 512: k * T + (tch + 1) * 512],
                                      start=(k == 0), stop=(k == KT - 1))
                    li.then_inc(_tE, 1)
                    M[("gumm", s, tch)] = tE.inc()
                    if tch == 1:
                        M[("gumm_last", s)] = tE.n

                block.tensor(_mm)

                def _ga(sc, s=s, tch=tch, bkG=bkG):
                    sc.wait_ge(_tE, M[("gumm", s, tch)])
                    if ("guv", s - 1, tch) in M:
                        sc.wait_ge(_vE, M[("guv", s - 1, tch)])
                    sc.activation(sgt[:, tch * 512:(tch + 1) * 512],
                                  bkG[:, 0:512], AF.Silu).then_inc(_aE, 1)
                    M[("gua", s, tch)] = aE.inc()

                block.scalar(_ga)

                def _gv(v, s=s, ft=ft, tch=tch, bkU=bkU, bi=bi):
                    v.wait_ge(_aE, M[("gua", s, tch)])
                    if ("dn_done", bi - 1) in M:
                        v.wait_ge(_tE, M[("dn_done", bi - 1)])
                    v.tensor_mul(gu[:, ft * T + tch * 512: ft * T + (tch + 1) * 512],
                                 bkU[:, 0:512],
                                 sgt[:, tch * 512:(tch + 1) * 512]).then_inc(_vE, 1)
                    M[("guv", s, tch)] = vE.inc()

                block.vector(_gv)

        def emit_down(bi):
            emit_wdl(bi, 0)
            emit_wdl(bi, 1)
            for dq in range(4):
                src = wdq0 if dq % 2 == 0 else wdq1
                for g in range(NTS):
                    ds = dq * NTS + g
                    bk = pb[4 + ds % 2]

                    def _mm(t, bi=bi, dq=dq, g=g, bk=bk, ds=ds, src=src):
                        if g == 0:
                            wq(t, M[("wdl", bi, dq)])
                        if dq == 0 and g == 0:
                            t.wait_ge(_vE, M[("guv", bi * NFT + NFT - 1, 1)])
                        if ds >= 2:
                            t.wait_ge(_vE, M[("dstt", bi, ds - 2)])
                        elif bi > 0:
                            t.wait_ge(_vE, M[("dstt", bi - 1, 30 + ds)])
                        for ftx in range(NFT):
                            li = t.matmul(bk[:, 0:512],
                                          gu[:, ftx * T + g * 128: ftx * T + g * 128 + 128],
                                          src[:, ftx * 512:(ftx + 1) * 512],
                                          start=(ftx == 0), stop=(ftx == NFT - 1))
                        li.then_inc(_tE, 1)
                        M[("dmm", bi, ds)] = tE.inc()
                        if g == NTS - 1:
                            M[("dmm_last", bi, dq)] = tE.n
                            if dq == 3:
                                M[("dn_done", bi)] = tE.n

                    block.tensor(_mm)

                    def _st(v, bi=bi, dq=dq, g=g, bk=bk, ds=ds):
                        v.wait_ge(_tE, M[("dmm", bi, ds)])
                        sl = moe[:, g * HID + dq * 512: g * HID + (dq + 1) * 512]
                        if bi == 0:
                            v.tensor_scalar(sl, bk[:, 0:512], scale_col(bi, g),
                                            None, ALU.mult).then_inc(_vE, 1)
                        else:
                            v.scalar_tensor_tensor(sl, bk[:, 0:512], scale_col(bi, g), sl,
                                                   ALU.mult, ALU.add).then_inc(_vE, 1)
                        M[("dstt", bi, ds)] = vE.inc()

                    block.vector(_st)
                if dq + 2 < 4:
                    emit_wdl(bi, dq + 2)

        for bi in range(NB):
            for ft in range(NFT):
                s = bi * NFT + ft
                emit_gul(s)
                emit_gu_step(s)
            emit_down(bi)
        for g in range(NTS):
            M[("moe_done", g)] = M[("dstt", NB - 1, 24 + g)]

        for g in range(NTS):
            def _xl(s, g=g):
                s.wait_ge(_vE, M[("moe_done", g)])
                s.wait_ge(_tE, M[("dn_done", NB - 1)])
                if ("fst", g - 1) in M:
                    wq(s, M[("fst", g - 1)])
                gi = QS.grp(s)
                s.dma_start(out=xt, in_=xnew_dram[g * 128:(g + 1) * 128, :]).then_inc(QS.hs[gi], 16)
                M[("fx", g)] = QS.tick(gi)

            def _fv(v, g=g):
                wq(v, M[("fx", g)])
                if ("fst", g - 1) in M:
                    wq(v, M[("fst", g - 1)])
                if g == 0:
                    v.wait_ge(_vE, M["bc_gb"])
                vss(v.tensor_mul(htm, moe[:, g * HID:(g + 1) * HID], bcP2), v)
                v.tensor_add(htm, htm, xt).then_inc(_vE, 1)
                M[("fv", g)] = vE.inc()

            def _fs(s, g=g):
                s.wait_ge(_vE, M[("fv", g)])
                gi = QS.grp(s)
                s.dma_start(out=out_d[g * 128:(g + 1) * 128, :], in_=htm).then_inc(QS.hs[gi], 16)
                M[("fst", g)] = QS.tick(gi)

            block.sync(_xl)
            block.vector(_fv)
            block.sync(_fs)

        def _wait_all(s):
            for mk in QS.finals() + QG.finals():
                wq(s, mk)

        block.sync(_wait_all)

    return nc, tap_d


# version 11 (final)
# speedup vs baseline: 1.1005x; 1.1005x over previous
import numpy as np
import concourse.bass as bass
import concourse.mybir as mybir
from concourse.bass_utils import run_bass_kernel_spmd

F32 = mybir.dt.float32
BF16 = mybir.dt.bfloat16
AF = mybir.ActivationFunctionType
ALU = mybir.AluOpType
AX = mybir.AxisListType

HID = 2048
INTER = 4096
NH = 64
HP = 64
NS = 128
KCONV = 4
CH = 256
CONV_DIM = INTER + 2 * NS
PROJ = INTER + CONV_DIM + NH
FMOE = 1408
FSH = 5632
NEXP = 8
EPS_LN = 1e-5
EPS_RMS = 1e-5

T = 1024
NCH = 4
NTS = 8
KT = HID // 128
KI = INTER // 128
GW = 256
NGRP = CONV_DIM // GW


class Ctr:
    def __init__(self, h):
        self.h = h
        self.n = 0

    def inc(self, k=1):
        self.n += k
        return self.n


class DmaPool:

    def __init__(self, handles):
        self.hs = list(handles)
        self.cnt = [0] * len(self.hs)
        self.rr = 0

    def grp(self, eng):
        i = self.rr
        self.rr = (self.rr + 1) % len(self.hs)
        if self.cnt[i] > 0:
            eng.wait_ge(self.hs[i], self.cnt[i])
        return i

    def tick(self, i, k=16):
        self.cnt[i] += k
        return (self.hs[i], self.cnt[i])

    def mark(self, i):
        return (self.hs[i], self.cnt[i])

    def finals(self):
        return [(h, c) for h, c in zip(self.hs, self.cnt) if c > 0]


def wq(eng, mark):
    eng.wait_ge(mark[0], mark[1])


def kernel(**inputs):
    in_maps = host_prep(inputs)
    nc, _ = build_nc()
    res = run_bass_kernel_spmd(nc, in_maps, core_ids=list(range(8)))
    out = np.empty((4, 2048, HID), np.float32)
    for c in range(8):
        b, hf = c // 2, c % 2
        out[b, hf * T:(hf + 1) * T, :] = res.results[c]["out"]
    return out


def host_prep(inputs):
    f32 = lambda k: np.ascontiguousarray(np.asarray(inputs[k], np.float32))
    x = f32("L0_attn_input_states")
    cT = np.ascontiguousarray(f32("L0_c").T)
    wmod = f32("L0_wmod")
    shared = dict(
        w_inT=np.ascontiguousarray(f32("L0_attn_in_proj_w").T),
        w_outT=np.ascontiguousarray(f32("L0_attn_out_proj_w").T),
        convw=np.ascontiguousarray(f32("L0_attn_conv_w").reshape(34, 128, KCONV)
                                   .transpose(1, 0, 2).reshape(128, 34 * KCONV)),
        convb=np.ascontiguousarray(f32("L0_attn_conv_b").reshape(34, 128).T),
        dtb=np.ascontiguousarray(f32("L0_attn_dt_bias").reshape(1, NH)),
        alog=np.ascontiguousarray(f32("L0_attn_A_log").reshape(1, NH)),
        dvec=np.ascontiguousarray(f32("L0_attn_D").reshape(1, NH)),
        normw=np.ascontiguousarray(f32("L0_attn_norm_w").reshape(1, INTER)),
        wr=np.ascontiguousarray(f32("L0_ffn_Wr").reshape(KT, 128, NEXP)
                                .transpose(1, 0, 2).reshape(128, KT * NEXP)),
        wsg=np.ascontiguousarray(f32("L0_ffn_Wsg").reshape(KT, 128).T),
        wgT=np.ascontiguousarray(f32("L0_ffn_Wg").reshape(NEXP, KT, 128, 11, 128)
                                 .transpose(0, 3, 2, 1, 4).reshape(NEXP, 11, 128, KT * 128)),
        wuT=np.ascontiguousarray(f32("L0_ffn_Wu").reshape(NEXP, KT, 128, 11, 128)
                                 .transpose(0, 3, 2, 1, 4).reshape(NEXP, 11, 128, KT * 128)),
        sgT=np.ascontiguousarray(f32("L0_ffn_Sg").reshape(KT, 128, 44, 128)
                                 .transpose(2, 1, 0, 3).reshape(44, 128, KT * 128)),
        suT=np.ascontiguousarray(f32("L0_ffn_Su").reshape(KT, 128, 44, 128)
                                 .transpose(2, 1, 0, 3).reshape(44, 128, KT * 128)),
        wd=f32("L0_ffn_Wd"), sd=f32("L0_ffn_Sd"),
    )
    in_maps = []
    for c in range(8):
        b, hf = c // 2, c % 2
        if hf == 1:
            xpre = np.ascontiguousarray(x[b, T - 3:T, :])
            premask = np.ones((4, 1), np.float32)
            msend = np.zeros((128, 1), np.float32)
            mrecv = np.ones((128, 1), np.float32)
        else:
            xpre = np.zeros((3, HID), np.float32)
            premask = np.zeros((4, 1), np.float32)
            msend = np.ones((128, 1), np.float32)
            mrecv = np.zeros((128, 1), np.float32)
        maskb = np.zeros((4, 1), np.float32)
        maskb[b, 0] = 1.0
        cstf = np.broadcast_to(np.array([EPS_LN, 1.0 / HID, -1.0 / HID, -0.5,
                                         EPS_RMS, 1.0 / INTER, 0.0, 0.0], np.float32),
                               (128, 8)).copy()
        m = dict(
            x=np.ascontiguousarray(x[b, hf * T:(hf + 1) * T, :]),
            xpre=xpre, premask=premask, msend=msend, mrecv=mrecv, maskb=maskb, cstf=cstf,
            ctsl=np.ascontiguousarray(cT[c * 256:(c + 1) * 256, :].reshape(2, 128, 4)
                                      .transpose(1, 0, 2).reshape(128, 8)),
            wmodsl=np.ascontiguousarray(wmod[c * 256:(c + 1) * 256, :]),
        )
        m.update(shared)
        in_maps.append(m)
    return in_maps


def build_nc(taps=(), upto="E"):
    nc = bass.Bass("TRN2", target_bir_lowering=False, debug=False)
    taps = set(taps)
    tap_d = {}

    def din(name, shape):
        return nc.dram_tensor(name, list(shape), F32, kind="ExternalInput").ap()

    x_d = din("x", (T, HID))
    xpre_d = din("xpre", (3, HID))
    premask_d = din("premask", (4, 1))
    msend_d = din("msend", (128, 1))
    mrecv_d = din("mrecv", (128, 1))
    maskb_d = din("maskb", (4, 1))
    cstf_d = din("cstf", (128, 8))
    ctsl_d = din("ctsl", (128, 8))
    wmodsl_d = din("wmodsl", (256, 12288))
    w_inT_d = din("w_inT", (HID, PROJ))
    w_outT_d = din("w_outT", (INTER, HID))
    convw_d = din("convw", (128, 34 * KCONV))
    convb_d = din("convb", (128, 34))
    dtb_d = din("dtb", (1, NH))
    alog_d = din("alog", (1, NH))
    dvec_d = din("dvec", (1, NH))
    normw_d = din("normw", (1, INTER))
    wr_d = din("wr", (128, KT * NEXP))
    wsg_d = din("wsg", (128, KT))
    wgT_d = din("wgT", (NEXP, 11, 128, KT * 128))
    wuT_d = din("wuT", (NEXP, 11, 128, KT * 128))
    wd_d = din("wd", (NEXP, FMOE, HID))
    sgT_d = din("sgT", (44, 128, KT * 128))
    suT_d = din("suT", (44, 128, KT * 128))
    sd_d = din("sd", (FSH, HID))
    out_d = nc.dram_tensor("out", [T, HID], F32, kind="ExternalOutput").ap()

    def mktap(name, shape):
        if name in taps:
            tap_d[name] = nc.dram_tensor("tap_" + name, list(shape), F32, kind="ExternalOutput").ap()
            return tap_d[name]
        return None

    t_h = mktap("h", (T, HID))
    t_hs = mktap("hs", (T, INTER))
    t_bc = mktap("bc", (2 * NS, T))
    t_dt = mktap("dt", (T, NH))
    t_y = mktap("y", (T, INTER))
    t_yn = mktap("yn", (T, INTER))
    t_c = mktap("c", (T, NEXP))
    t_hc = mktap("hc", (128, KT * T))

    w_inT3 = w_inT_d.rearrange("(k p) f -> p k f", p=128)
    w_outT3 = w_outT_d.rearrange("(i p) f -> p i f", p=128)
    mod_dram = nc.dram_tensor("mod_dram", [1, 6 * HID], F32, kind="Internal").ap()
    xnew_dram = nc.dram_tensor("xnew_dram", [T, HID], F32, kind="Internal").ap()
    ccin_mod = nc.dram_tensor("ccin_mod", [4, 6 * HID], F32, kind="Internal").ap()
    ccout_mod = nc.dram_tensor("ccout_mod", [4, 6 * HID], F32, kind="Internal").ap()
    ccin_st = nc.dram_tensor("ccin_st", [128, NH * HP], F32, kind="Internal").ap()
    ccout_st = nc.dram_tensor("ccout_st", [128, NH * HP], F32, kind="Internal").ap()

    SBASE = 16512

    def sat(name, offset, shape, dtype, **kw):
        return nc.alloc_sbuf_tensor_at(name, list(shape), dtype, offset=SBASE + offset, **kw).ap()

    yb = sat("yb", 0, [128, NTS * INTER], BF16)
    wmod_sb = sat("wmod_sb", 0, [128, 2 * 12288], BF16)
    modrow = sat("modrow", 0, [1, 6 * HID], F32)
    wot = sat("wot", 0, [128, 2 * KI * 256], BF16)
    moe = sat("moe", 0, [128, NTS * HID], F32)
    B2 = 65536
    hs_tm = sat("hs_tm", B2, [128, 2 * INTER], BF16)
    h_fm = sat("h_fm", B2 + 16384, [128, KT * 256], BF16)
    wtg = sat("wtg", B2 + 24576, [128, 2 * KT * GW], BF16)
    modpart = sat("modpart", B2, [4, 12288], F32)
    modfull = sat("modfull", B2, [4, 4096], F32)
    sstage = sat("sstage", B2, [128, NH * HP], F32)
    hfmC = sat("hfmC", B2, [128, KT * T], BF16)
    wtgC = sat("wtgC", B2 + 32768, [128, 2 * KT * 256], BF16)
    ssqp = sat("ssqp", B2 + 49152, [128, 128], F32)
    ynf = sat("ynf", B2, [128, KI * T], BF16)
    h2f = sat("h2f", B2, [128, KT * T], BF16)
    gu = sat("gu", B2 + 32768, [128, 11 * T], BF16)
    sgt = sat("sgt", B2 + 55296, [128, T], BF16)
    SOFF = 131072
    S = sat("S", SOFF, [128, NH * HP], F32)
    BCO = SOFF + 16384
    bcP0 = sat("bcP0", BCO, [128, HID], BF16)
    bcP1 = sat("bcP1", BCO + 4096, [128, HID], BF16)
    bcP2 = sat("bcP2", BCO + 8192, [128, HID], BF16)
    SC = BCO + 12288
    xt = sat("xt", SC, [128, HID], F32)
    htm = sat("htm", SC + 8192, [128, HID], F32)
    scr2 = sat("scr2", SC + 16384, [128, 1024], F32)
    r2B = sat("r2B", SC + 16384, [128, 512], F32)
    e_sbB = sat("e_sbB", SC + 16384 + 2048, [128, 512], BF16)
    mt_sbB = sat("mt_sbB", SC + 16384 + 3072, [128, 512], BF16)
    hs_ssdB = sat("hs_ssdB", BCO + 8192, [128, 128], BF16)
    bdecB = sat("bdecB", BCO + 8192 + 256, [128, 256], BF16)
    s16tB = sat("s16tB", BCO + 8192 + 768, [128, HP], BF16)
    yoB = sat("yoB", BCO + 8192 + 896, [128, 128], F32)
    y1B = sat("y1B", BCO + 8192 + 1408, [128, 128], F32)
    WS = SC + 20480
    wgt2 = sat("wgt2", WS, [128, 2 * KT * 128], BF16)
    wut2 = sat("wut2", WS + 8192, [128, 2 * KT * 128], BF16)
    normwB = sat("normwB", WS, [128, INTER], BF16)
    wdq0 = sat("wdq0", SOFF, [128, 11 * 512], BF16)
    wdq1 = sat("wdq1", SC, [128, 11 * 512], BF16)
    SM = WS + 22528
    _off = [SM]

    def sal(name, shape, dtype):
        t = sat(name, _off[0], shape, dtype)
        nbytes = int(np.prod(shape[1:])) * (2 if dtype == BF16 else 4)
        _off[0] += (nbytes + 31) // 32 * 32
        return t

    _offab = [B2 + 40960]

    def sab(name, shape, dtype):
        t = sat(name, _offab[0], shape, dtype)
        nbytes = int(np.prod(shape[1:])) * (2 if dtype == BF16 else 4)
        _offab[0] += (nbytes + 31) // 32 * 32
        return t

    cm_fm = sab("cm_fm", [128, T], BF16)
    bm_fm = sab("bm_fm", [128, 256], BF16)
    bm_tm = sab("bm_tm", [128, 256], BF16)
    sdo = sab("sdo", [128, 512], F32)
    acum = sab("acum", [128, 128], F32)
    acum_hm = sab("acum_hm", [64, 256], F32)
    dt_tm = sab("dt_tm", [128, 128], F32)
    adt_tm = sab("adt_tm", [128, 128], F32)
    trow = sab("trow", [1, 256], F32)
    xctail = sab("xctail", [128, 102], BF16)
    mask01 = sab("mask01", [128, 512], BF16)
    e_sb = sab("e_sb", [128, 512], BF16)
    gmm = sab("gmm", [128, 512], BF16)
    mt_sb = sab("mt_sb", [128, 512], BF16)
    hs_ssd = sab("hs_ssd", [128, 128], BF16)
    bdec = sab("bdec", [128, 256], BF16)
    decayb = sab("decayb", [128, 128], F32)
    dsub = sab("dsub", [128, 128], F32)
    yo = sab("yo", [128, 128], F32)
    y1 = sab("y1", [128, 128], F32)
    r2 = sab("r2", [128, 512], F32)
    s16t = sab("s16t", [128, HP], BF16)
    wdtT = sab("wdtT", [128, KT * NH], BF16)
    hpre = sat("hpre", 49152, [3, HID], F32)
    hpre_fm = sab("hpre_fm", [128, KT * 3], BF16)
    dts1 = sab("dts1", [128, 64], F32)
    dts2 = sab("dts2", [128, 64], F32)
    xc_sb = sab("xc_sb", [128, 2 * 260], BF16)
    diag = sab("diag", [128, 8 * 128], BF16)
    convw_sb = sab("convw_sb", [128, 34 * KCONV], F32)
    convb_sb = sab("convb_sb", [128, 34], F32)
    assert _offab[0] <= B2 + 65536, f"AB zone overflow: {_offab[0] - B2}"

    dtbB = sal("dtbB", [128, NH], F32)
    AB = sal("AB", [128, NH], F32)
    DB = sal("DB", [128, NH], F32)
    tcB = sal("tcB", [128, NH], F32)
    etcB = sal("etcB", [128, NH], F32)
    iden = sal("iden", [128, 128], BF16)
    idf = sal("idf", [128, 128], F32)
    t01f = sal("t01f", [128, 128], F32)
    onesf = sal("onesf", [128, 128], F32)
    c_sb = sal("c_sb", [128, NEXP * NTS], F32)
    sig_sb = sal("sig_sb", [128, NTS], F32)
    st8 = sal("st8", [128, 16], F32)
    silu_ct = sal("silu_ct", [128, 8], BF16)
    maskb_sb = sal("maskb_sb", [4, 1], F32)
    premask_sb = sal("premask_sb", [4, 1], F32)
    msend_sb = sal("msend_sb", [128, 1], F32)
    mrecv_sb = sal("mrecv_sb", [128, 1], F32)
    wr_sb = sal("wr_sb", [128, KT * NEXP], BF16)
    wsg_sb = sal("wsg_sb", [128, KT], BF16)
    rowst = sal("rowst", [1, 512], F32)
    cst = sal("cst", [128, 8], F32)
    evrow = sal("evrow", [1, 2 * NH], F32)
    assert _off[0] <= 224 * 1024, f"SBUF overflow: {_off[0]}"

    pb = [nc.alloc_psum_tensor(f"pb{i}", [128, 512], F32).ap() for i in range(7)]
    pt0 = nc.alloc_psum_tensor("pt0", [128, 1024], BF16).ap()

    import contextlib
    with contextlib.ExitStack() as _stk:
        _qs_h = [_stk.enter_context(nc.semaphore(f"qs{i}")) for i in range(8)]
        _qg_h = [_stk.enter_context(nc.semaphore(f"qg{i}")) for i in range(8)]
        _tE = _stk.enter_context(nc.semaphore("tE"))
        _vE = _stk.enter_context(nc.semaphore("vE"))
        _aE = _stk.enter_context(nc.semaphore("aE"))
        _gp = _stk.enter_context(nc.semaphore("gp"))
        _cc = _stk.enter_context(nc.semaphore("cc"))
        block = _stk.enter_context(nc.Block())
        QS, QG = DmaPool(_qs_h), DmaPool(_qg_h)
        tE, vE, aE, gp = Ctr(_tE), Ctr(_vE), Ctr(_aE), Ctr(_gp)
        cc = Ctr(_cc)
        M = {}

        def vss(inst, v):
            inst.then_inc(_vE, 1)
            v.wait_ge(_vE, vE.inc())

        def ass(inst, sc):
            inst.then_inc(_aE, 1)
            sc.wait_ge(_aE, aE.inc())

        def p0_gpsimd(g):
            gi = QG.grp(g)
            g.dma_start(out=wmod_sb[:, 0:12288], in_=wmodsl_d[0:128, :]).then_inc(QG.hs[gi], 16)
            QG.tick(gi)
            g.dma_start(out=wmod_sb[:, 12288:24576], in_=wmodsl_d[128:256, :]).then_inc(QG.hs[gi], 16)
            QG.tick(gi)
            g.dma_start(out=silu_ct, in_=ctsl_d).then_inc(QG.hs[gi], 16)
            M["wmod"] = QG.tick(gi)
            gi = QG.grp(g)
            for dst, src in ((maskb_sb, maskb_d), (premask_sb, premask_d), (cst, cstf_d),
                             (msend_sb, msend_d), (mrecv_sb, mrecv_d),
                             (convb_sb, convb_d),
                             (convw_sb, convw_d),
                             (scr2[0:1, 0:64], dtb_d), (scr2[0:1, 64:128], alog_d),
                             (scr2[0:1, 128:192], dvec_d), (hpre, xpre_d),
                             (wr_sb, wr_d),
                             (wsg_sb, wsg_d)):
                g.dma_start(out=dst, in_=src).then_inc(QG.hs[gi], 16)
                QG.tick(gi)
            M["smalls"] = QG.mark(gi)
            g.memset(onesf, 1.0)
            g.memset(iden, 1.0)
            g.memset(t01f, 1.0)
            g.memset(mask01, 1.0)
            g.memset(evrow, 0.0)
            g.memset(S, 0.0).then_inc(_gp, 1)
            g.wait_ge(_gp, gp.inc())
            g.affine_select(iden, iden, pattern=[[1, 128]], compare_op=ALU.is_equal,
                            fill=0.0, base=0, channel_multiplier=-1)
            g.affine_select(t01f, t01f, pattern=[[1, 128]], compare_op=ALU.is_ge,
                            fill=0.0, base=0, channel_multiplier=-1)
            g.affine_select(mask01[:, 0:256], mask01[:, 0:256], pattern=[[1, 256]],
                            compare_op=ALU.is_ge, fill=0.0, base=0, channel_multiplier=-1)
            g.affine_select(mask01[:, 256:512], mask01[:, 256:512], pattern=[[1, 256]],
                            compare_op=ALU.is_ge, fill=0.0, base=-128,
                            channel_multiplier=-1).then_inc(_gp, 1)
            g.wait_ge(_gp, gp.inc())
            g.tensor_copy(idf, iden).then_inc(_gp, 1)
            M["consts"] = gp.inc()
            gi = QG.grp(g)
            for kq in range(4):
                g.dma_start(out=wdtT[:, kq * 4 * NH:(kq + 1) * 4 * NH].rearrange(
                                "p (k f) -> p k f", k=4),
                            in_=w_inT3[:, kq * 4:(kq + 1) * 4, INTER + CONV_DIM:PROJ]
                            ).then_inc(QG.hs[gi], 16)
                QG.tick(gi)
            M["wdtT"] = QG.mark(gi)

        block.gpsimd(p0_gpsimd)

        def p0_scalar(sc):
            wq(sc, M["wmod"])
            ass(sc.activation(silu_ct, silu_ct, AF.Silu), sc)
            M["siluct"] = aE.n
            wq(sc, M["smalls"])
            ass(sc.activation(scr2[0:1, 64:128], scr2[0:1, 64:128], AF.Exp), sc)
            M["expalog"] = aE.n

        block.scalar(p0_scalar)

        for j in range(24):
            def _mt(t, j=j):
                if j == 0:
                    t.wait_ge(_aE, M["siluct"])
                bk = pb[j % 2]
                if j >= 2:
                    t.wait_ge(_vE, M[("modev", j - 2)])
                for a in range(2):
                    mm = t.matmul(bk[0:4, :], silu_ct[:, 4 * a:4 * a + 4],
                                  wmod_sb[:, a * 12288 + j * 512: a * 12288 + (j + 1) * 512],
                                  start=(a == 0), stop=(a == 1))
                mm.then_inc(_tE, 1)
                M[("modmm", j)] = tE.inc()

            def _mv(v, j=j):
                v.wait_ge(_tE, M[("modmm", j)])
                v.tensor_copy(modpart[:, j * 512:(j + 1) * 512], pb[j % 2][0:4, :]).then_inc(_vE, 1)
                M[("modev", j)] = vE.inc()

            block.tensor(_mt)
            block.vector(_mv)
        M["modpart_done"] = vE.n

        def p0_cc(g):
            g.wait_ge(_vE, M["modpart_done"])
            gi = QG.grp(g)
            g.dma_start(out=ccin_mod, in_=modpart).then_inc(QG.hs[gi], 16)
            wq(g, QG.tick(gi))
            g.collective_compute("AllReduce", ALU.add, replica_groups=[list(range(8))],
                                 ins=[ccin_mod], outs=[ccout_mod]).then_inc(_cc, 1)
            cc.inc()
            g.wait_ge(_cc, cc.n)

        block.gpsimd(p0_cc)

        for part in range(3):
            def _ld(g, part=part):
                if part > 0:
                    g.wait_ge(_vE, M[("rowev", part - 1, 7)])
                gi = QG.grp(g)
                g.dma_start(out=modfull[:, 0:4096],
                            in_=ccout_mod[:, part * 4096:(part + 1) * 4096]).then_inc(QG.hs[gi], 16)
                M[("modfull", part)] = QG.tick(gi)

            block.gpsimd(_ld)

            for j in range(8):
                def _mm(t, part=part, j=j):
                    if j == 0:
                        wq(t, M[("modfull", part)])
                        if part == 0:
                            wq(t, M["smalls"])
                    bk = pb[2 + (j % 2)]
                    if j >= 2:
                        t.wait_ge(_vE, M[("rowev", part, j - 2)])
                    elif part > 0:
                        t.wait_ge(_vE, M[("rowev", part - 1, 6 + j)])
                    t.matmul(bk[0:1, :], maskb_sb, modfull[:, j * 512:(j + 1) * 512],
                             start=True, stop=True).then_inc(_tE, 1)
                    M[("rowmm", part, j)] = tE.inc()

                def _ev(v, part=part, j=j):
                    v.wait_ge(_tE, M[("rowmm", part, j)])
                    v.tensor_copy(modrow[:, part * 4096 + j * 512: part * 4096 + (j + 1) * 512],
                                  pb[2 + (j % 2)][0:1, :]).then_inc(_vE, 1)
                    M[("rowev", part, j)] = vE.inc()

                block.tensor(_mm)
                block.vector(_ev)
        M["modrow_done"] = vE.n

        def p0c_vector(v):
            v.wait_ge(_vE, M["modrow_done"])
            for base in (2 * HID, 5 * HID):
                gaap = modrow[:, base:base + HID]
                vss(v.tensor_scalar(xt[0:1, :], gaap, 0.0, None, ALU.is_ge), v)
                vss(v.tensor_scalar(xt[0:1, :], xt[0:1, :], 2.0, -1.0, ALU.mult, ALU.add), v)
                vss(v.tensor_scalar(htm[0:1, :], gaap, -1.0, None, ALU.mult), v)
                vss(v.tensor_max(htm[0:1, :], htm[0:1, :], gaap), v)
                vss(v.tensor_scalar_max(htm[0:1, :], htm[0:1, :], 0.25), v)
                vss(v.tensor_mul(gaap, xt[0:1, :], htm[0:1, :]), v)
            M["clamped"] = vE.n

        block.vector(p0c_vector)

        def p0c_sync(s):
            s.wait_ge(_vE, M["clamped"])
            gi = QS.grp(s)
            s.dma_start(out=mod_dram, in_=modrow).then_inc(QS.hs[gi], 16)
            M["mod_dram"] = QS.tick(gi)

        block.sync(p0c_sync)

        def emit_bcast(tag, dst, src_row, ncols, add1=False, bank=6, pre_tmark=None):
            nj = (ncols + 511) // 512
            prev_ev = vE.n
            for j in range(nj):
                def _t(t, j=j):
                    if j == 0:
                        t.wait_ge(_vE, prev_ev)
                        if pre_tmark is not None:
                            for mm in pre_tmark:
                                t.wait_ge(mm[0], mm[1])
                    w = min(512, ncols - j * 512)
                    if j >= 1:
                        t.wait_ge(_vE, M[(tag, "ev", j - 1)])
                    t.matmul(pb[bank][:, 0:w], onesf[0:1, :], src_row[:, j * 512:j * 512 + w],
                             start=True, stop=True).then_inc(_tE, 1)
                    M[(tag, "mm", j)] = tE.inc()

                def _v(v, j=j):
                    w = min(512, ncols - j * 512)
                    v.wait_ge(_tE, M[(tag, "mm", j)])
                    if add1:
                        v.tensor_scalar(dst[:, j * 512:j * 512 + w], pb[bank][:, 0:w],
                                        1.0, None, ALU.add).then_inc(_vE, 1)
                    else:
                        v.tensor_copy(dst[:, j * 512:j * 512 + w], pb[bank][:, 0:w]).then_inc(_vE, 1)
                    M[(tag, "ev", j)] = vE.inc()

                block.tensor(_t)
                block.vector(_v)
            M[tag] = vE.n

        emit_bcast("bc_ca", bcP0, modrow[:, HID:2 * HID], HID, add1=True,
                   pre_tmark=[(_vE, M["clamped"]), (_gp, M["consts"])])
        emit_bcast("bc_sa", bcP1, modrow[:, 0:HID], HID, add1=False)

        def p0d_tensor2(t):
            t.wait_ge(_aE, M["expalog"])
            t.wait_ge(_vE, M["bc_sa"])
            t.matmul(pb[6][:, 0:64], onesf[0:1, :], scr2[0:1, 0:64], start=True, stop=True)
            t.matmul(pb[6][:, 64:128], onesf[0:1, :], scr2[0:1, 64:128], start=True, stop=True)
            t.matmul(pb[6][:, 128:192], onesf[0:1, :], scr2[0:1, 128:192],
                     start=True, stop=True).then_inc(_tE, 1)
            M["rows3"] = tE.inc()

        def p0d_vector2(v):
            v.wait_ge(_tE, M["rows3"])
            v.tensor_copy(dtbB, pb[6][:, 0:64])
            v.tensor_scalar(AB, pb[6][:, 64:128], -1.0, None, ALU.mult)
            v.tensor_copy(DB, pb[6][:, 128:192]).then_inc(_vE, 1)
            M["rows3v"] = vE.inc()

        block.tensor(p0d_tensor2)
        block.vector(p0d_vector2)

        def emit_ln(tag, vsrc, n_part, dst_tm, waits_v=()):
            def _v(v):
                for wk in waits_v:
                    v.wait_ge(wk[0], wk[1])
                v.reduce_sum(st8[0:n_part, 0:1], vsrc, axis=AX.X)
                v.scalar_tensor_tensor(dst_tm, vsrc, 1.0, vsrc, ALU.mult, ALU.mult,
                                       accum_out=st8[0:n_part, 1:2]).then_inc(_vE, 1)
                M[(tag, "stat")] = vE.inc()

            def _s(sc):
                sc.wait_ge(_vE, M[(tag, "stat")])
                wq(sc, M["smalls"])
                sc.activation(st8[0:n_part, 2:3], st8[0:n_part, 0:1], AF.Square,
                              scale=cst[0:n_part, 1:2])
                sc.activation(st8[0:n_part, 3:4], st8[0:n_part, 0:1], AF.Copy,
                              scale=cst[0:n_part, 2:3]).then_inc(_aE, 1)
                M[(tag, "sc")] = aE.inc()

            def _v2(v):
                v.wait_ge(_aE, M[(tag, "sc")])
                v.tensor_scalar(st8[0:n_part, 4:5], st8[0:n_part, 1:2], 1.0 / HID,
                                st8[0:n_part, 2:3], ALU.mult, ALU.subtract).then_inc(_vE, 1)
                M[(tag, "var")] = vE.inc()

            def _s2(sc):
                sc.wait_ge(_vE, M[(tag, "var")])
                ass(sc.activation(st8[0:n_part, 5:6], st8[0:n_part, 4:5], AF.Ln,
                                  bias=cst[0:n_part, 0:1]), sc)
                sc.activation(st8[0:n_part, 6:7], st8[0:n_part, 5:6], AF.Exp,
                              scale=cst[0:n_part, 3:4]).then_inc(_aE, 1)
                M[(tag, "std")] = aE.inc()

            def _v3(v):
                v.wait_ge(_aE, M[(tag, "std")])
                vss(v.tensor_scalar(vsrc, vsrc, st8[0:n_part, 3:4], st8[0:n_part, 6:7],
                                    ALU.add, ALU.mult), v)
                vss(v.tensor_mul(dst_tm, vsrc, bcP0[0:n_part, :]), v)
                v.tensor_add(dst_tm, dst_tm, bcP1[0:n_part, :]).then_inc(_vE, 1)
                M[(tag, "h")] = vE.inc()

            block.vector(_v)
            block.scalar(_s)
            block.vector(_v2)
            block.scalar(_s2)
            block.vector(_v3)

        def p0e(g=None):
            emit_ln("pre", hpre, 3, htm[0:3, :], waits_v=[(_vE, M["bc_ca"]),
                                                          (_vE, M["bc_sa"]), M["smalls"]])

            def _vm(v):
                v.wait_ge(_vE, M[("pre", "h")])
                v.tensor_scalar(hpre, hpre, premask_sb[0:3, 0:1], None, ALU.mult).then_inc(_vE, 1)
                M["hpre_m"] = vE.inc()

            block.vector(_vm)

            def _t(t):
                t.wait_ge(_vE, M["hpre_m"])
                for kb in range(KT):
                    t.transpose(pb[4][:, (kb % 4) * 3:(kb % 4) * 3 + 3],
                                hpre[0:3, kb * 128:(kb + 1) * 128], idf[0:3, 0:3])
                    if kb % 4 == 3:
                        t.nop().then_inc(_tE, 1)
                        M[("preT", kb // 4)] = tE.inc()
                        if kb < KT - 1:
                            t.wait_ge(_vE, M[("preTev", kb // 4)] if False else 0)

            block.tensor(_t)

            def _t2v():
                for q in range(4):
                    def _v(v, q=q):
                        v.wait_ge(_tE, M[("preT", q)])
                        for kb in range(q * 4, q * 4 + 4):
                            v.tensor_copy(hpre_fm[:, kb * 3:(kb + 1) * 3],
                                          pb[4][:, (kb % 4) * 3:(kb % 4) * 3 + 3])
                        v.nop().then_inc(_vE, 1)
                        M[("preTev", q)] = vE.inc()
                    block.vector(_v)

            _t2v()

        def p0e_fixed():
            emit_ln("pre", hpre, 3, htm[0:3, :], waits_v=[(_vE, M["bc_ca"]),
                                                          (_vE, M["bc_sa"])])

            def _vm(v):
                v.tensor_scalar(hpre, hpre, premask_sb[0:3, 0:1], None, ALU.mult).then_inc(_vE, 1)
                M["hpre_m"] = vE.inc()

            block.vector(_vm)
            for q in range(4):
                def _t(t, q=q):
                    if q == 0:
                        t.wait_ge(_vE, M["hpre_m"])
                    else:
                        t.wait_ge(_vE, M[("preTev", q - 1)])
                    for kk in range(4):
                        kb = q * 4 + kk
                        li = t.transpose(pb[4][:, kk * 3:kk * 3 + 3],
                                         hpre[0:3, kb * 128:(kb + 1) * 128], idf[0:3, 0:3])
                    li.then_inc(_tE, 1)
                    M[("preT", q)] = tE.inc()

                def _v(v, q=q):
                    v.wait_ge(_tE, M[("preT", q)])
                    for kk in range(4):
                        kb = q * 4 + kk
                        li = v.tensor_copy(hpre_fm[:, kb * 3:(kb + 1) * 3],
                                           pb[4][:, kk * 3:kk * 3 + 3])
                    li.then_inc(_vE, 1)
                    M[("preTev", q)] = vE.inc()

                block.tensor(_t)
                block.vector(_v)

        p0e_fixed()
        M["hpre_fm"] = vE.n

        def emit_stub_fin():
            def _fin0(s):
                s.wait_ge(_vE, vE.n)
                s.wait_ge(_aE, aE.n)
                s.wait_ge(_tE, tE.n)
                for mk in QS.finals() + QG.finals():
                    wq(s, mk)
                gi = QS.grp(s)
                for g2 in range(NTS):
                    s.dma_start(out=out_d[g2 * 128:(g2 + 1) * 128, :],
                                in_=onesf[:, 0:1].broadcast_to([128, HID])).then_inc(QS.hs[gi], 16)
                    QS.tick(gi)
                wq(s, QS.mark(gi))

            block.sync(_fin0)

        if upto == "P0":
            emit_stub_fin()
            return nc, tap_d

        def emit_hslice(tagpfx, g, dst_fm, fm_stride, x_src, extra_v_waits=(),
                        pe_free_mark=None, extra_s_waits=()):
            tag = (tagpfx, g)

            def _sy(s):
                if ("xtfree",) in M:
                    s.wait_ge(_vE, M[("xtfree",)])
                for wk in extra_s_waits:
                    wq(s, wk)
                gi = QS.grp(s)
                s.dma_start(out=xt, in_=x_src).then_inc(QS.hs[gi], 16)
                M[(tag, "x")] = QS.tick(gi)

            block.sync(_sy)
            waits = [M[(tag, "x")]] + list(extra_v_waits)
            if ("htmfree",) in M:
                waits.append((_tE, M[("htmfree",)]))
            emit_ln(tag, xt, 128, htm, waits_v=waits)
            M[("xtfree",)] = M[(tag, "h")]

            if t_h is not None and tagpfx == "hA":
                def _tap(s):
                    s.wait_ge(_vE, M[(tag, "h")])
                    gi = QS.grp(s)
                    s.dma_start(out=t_h[g * 128:(g + 1) * 128, :], in_=htm).then_inc(QS.hs[gi], 16)
                    M[(tag, "tap")] = QS.tick(gi)

                block.sync(_tap)

            for q in range(4):
                def _t(t, q=q):
                    if q == 0:
                        t.wait_ge(_vE, M[(tag, "h")])
                        if (tag, "tap") in M:
                            wq(t, M[(tag, "tap")])
                        if pe_free_mark is not None:
                            t.wait_ge(_vE, pe_free_mark)
                    bk = pb[5 if q % 2 == 0 else 6]
                    if (tag, "Tev", q - 2) in M:
                        t.wait_ge(_vE, M[(tag, "Tev", q - 2)])
                    elif ("pT_ev", 5 if q % 2 == 0 else 6) in M:
                        t.wait_ge(_vE, M.pop(("pT_ev", 5 if q % 2 == 0 else 6)))
                    for kk in range(4):
                        kb = q * 4 + kk
                        li = t.transpose(bk[:, kk * 128:(kk + 1) * 128],
                                         htm[:, kb * 128:(kb + 1) * 128], idf)
                    li.then_inc(_tE, 1)
                    M[(tag, "T", q)] = tE.inc()

                def _v(v, q=q):
                    v.wait_ge(_tE, M[(tag, "T", q)])
                    bk = pb[5 if q % 2 == 0 else 6]
                    for kk in range(4):
                        kb = q * 4 + kk
                        col = kb * fm_stride + (g % (fm_stride // 128)) * 128
                        li = v.tensor_copy(dst_fm[:, col:col + 128], bk[:, kk * 128:(kk + 1) * 128])
                    li.then_inc(_vE, 1)
                    M[(tag, "Tev", q)] = vE.inc()

                block.tensor(_t)
                block.vector(_v)
            M[("htmfree",)] = M[(tag, "T", 3)]
            M[("pT_ev", 5)] = M[(tag, "Tev", 2)]
            M[("pT_ev", 6)] = M[(tag, "Tev", 3)]
            return M[(tag, "Tev", 3)]

        def emit_chunk_inproj(icu):
            pfx = icu == 0
            for g in range(NGRP):
                buf = g % 2
                tagw = ("wtg", icu, g)

                def _stage(s, g=g, buf=buf, tagw=tagw):
                    key = ("wtgmm", icu, g - 2) if g >= 2 else \
                        (("wtgmm", icu - 1, 16 if buf == 0 else 15) if icu > 0 else None)
                    if key and key in M:
                        s.wait_ge(_tE, M[key])
                    r0 = 4096 + g * GW
                    gi = QG.grp(s)
                    for kq in range(4):
                        s.dma_start(out=wtg[:, buf * 4096 + kq * 4 * GW:
                                            buf * 4096 + (kq + 1) * 4 * GW].rearrange(
                                        "p (k f) -> p k f", k=4),
                                    in_=w_inT3[:, kq * 4:(kq + 1) * 4,
                                               r0:r0 + GW]).then_inc(QG.hs[gi], 16)
                        QG.tick(gi)
                    M[tagw] = QG.mark(gi)

                block.gpsimd(_stage)

                def _mm1(t, ct, buf=buf, tagw=tagw):
                    if ct % 2 == 0:
                        wq(t, M[tagw])
                    if ct == 0:
                        t.wait_ge(_vE, M[("hfm_done", icu)])
                        if pfx:
                            t.wait_ge(_vE, M["hpre_fm"])
                        if icu > 0:
                            t.wait_ge(_vE, M[("ssdv", icu - 1, NH - 1)])
                    mt = ct % 2
                    bk = pb[ct % 2]
                    key = ("xcev", icu, ct - 2) if ct >= 2 else \
                        (("xcev", icu - 1, 32 + ct) if icu > 0 else None)
                    if key and key in M:
                        t.wait_ge(_vE, M[key])
                    for k in range(KT):
                        li = t.matmul(bk[:, 0:256],
                                      wtg[:, buf * 4096 + k * GW + mt * 128:
                                          buf * 4096 + k * GW + mt * 128 + 128],
                                      h_fm[:, k * 256:(k + 1) * 256],
                                      start=(k == 0), stop=(k == KT - 1))
                    if pfx:
                        if ("xcev", icu, ct - 1) in M:
                            t.wait_ge(_vE, M[("xcev", icu, ct - 1)])
                        for k in range(KT):
                            li = t.matmul(pb[4][:, 0:3],
                                          wtg[:, buf * 4096 + k * GW + mt * 128:
                                              buf * 4096 + k * GW + mt * 128 + 128],
                                          hpre_fm[:, k * 3:(k + 1) * 3],
                                          start=(k == 0), stop=(k == KT - 1))
                    li.then_inc(_tE, 1)
                    M[("xcmm", icu, ct)] = tE.inc()
                    if ct % 2 == 1:
                        M[("wtgmm", icu, g)] = tE.n

                def _evac1(v, ct):
                    bk = pb[ct % 2]
                    b = ct % 2
                    v.wait_ge(_tE, M[("xcmm", icu, ct)])
                    key = ("cvmm", icu, ct - 2) if ct >= 2 else \
                        (("cvmm", icu - 1, 32 + ct) if icu > 0 else None)
                    if key and key in M:
                        v.wait_ge(_tE, M[key])
                    if icu == 0 and ct == 0:
                        wq(v, M["smalls"])
                    if pfx:
                        v.tensor_copy(xc_sb[:, b * 260:b * 260 + 3], pb[4][:, 0:3])
                    else:
                        v.tensor_copy(xc_sb[:, b * 260:b * 260 + 3], xctail[:, ct * 3:ct * 3 + 3])
                    vss(v.tensor_copy(xc_sb[:, b * 260 + 3:b * 260 + 259], bk[:, 0:256]), v)
                    v.tensor_copy(xctail[:, ct * 3:ct * 3 + 3],
                                  xc_sb[:, b * 260 + 256:b * 260 + 259])
                    for k in range(KCONV):
                        li = v.tensor_scalar(diag[:, (b * 4 + k) * 128:(b * 4 + k + 1) * 128], iden,
                                             convw_sb[:, ct * KCONV + k:ct * KCONV + k + 1],
                                             None, ALU.mult)
                    li.then_inc(_vE, 1)
                    M[("xcev", icu, ct)] = vE.inc()

                def _conv1(t, ct):
                    b = ct % 2
                    bk = pb[2 + (ct % 2)]
                    t.wait_ge(_vE, M[("xcev", icu, ct)])
                    key = ("cvac", icu, ct - 2) if ct >= 2 else \
                        (("cvac", icu - 1, 32 + ct) if icu > 0 else None)
                    if key and key in M:
                        t.wait_ge(_aE, M[key])
                    for k in range(KCONV):
                        li = t.matmul(bk[:, 0:256], diag[:, (b * 4 + k) * 128:(b * 4 + k + 1) * 128],
                                      xc_sb[:, b * 260 + k:b * 260 + k + 256],
                                      start=(k == 0), stop=(k == KCONV - 1))
                    li.then_inc(_tE, 1)
                    M[("cvmm", icu, ct)] = tE.inc()

                def _cvevac1(sc, ct):
                    bk = pb[2 + (ct % 2)]
                    sc.wait_ge(_tE, M[("cvmm", icu, ct)])
                    if icu == 0 and ct == 0:
                        wq(sc, M["smalls"])
                    if ct < 32:
                        if ("hsT", icu, ct - 1) in M:
                            sc.wait_ge(_tE, M[("hsT", icu, ct - 1)])
                        elif icu > 0 and ("hsT", icu - 1, 31) in M:
                            sc.wait_ge(_tE, M[("hsT", icu - 1, 31)])
                        sc.activation(e_sb[:, 0:256], bk[:, 0:256], AF.Silu,
                                      bias=convb_sb[:, ct:ct + 1]).then_inc(_aE, 1)
                    elif ct == 32:
                        sc.activation(bm_fm[:, 0:256], bk[:, 0:256], AF.Silu,
                                      bias=convb_sb[:, ct:ct + 1]).then_inc(_aE, 1)
                    else:
                        sc.activation(cm_fm[:, icu * 256:(icu + 1) * 256], bk[:, 0:256],
                                      AF.Silu, bias=convb_sb[:, ct:ct + 1]).then_inc(_aE, 1)
                    M[("cvac", icu, ct)] = aE.inc()

                def _hsT1(t, ct):
                    if ct >= 32:
                        return
                    t.wait_ge(_aE, M[("cvac", icu, ct)])
                    if ("hsTev", icu, ct - 1) in M:
                        t.wait_ge(_vE, M[("hsTev", icu, ct - 1)])
                    elif icu > 0 and ("hsTev", icu - 1, 31) in M:
                        t.wait_ge(_vE, M[("hsTev", icu - 1, 31)])
                    for sl in range(2):
                        li = t.transpose(pt0[:, sl * 128:(sl + 1) * 128],
                                         e_sb[:, sl * 128:sl * 128 + 128], iden)
                    li.then_inc(_tE, 1)
                    M[("hsT", icu, ct)] = tE.inc()

                def _hsTev1(v, ct):
                    if ct >= 32:
                        return
                    v.wait_ge(_tE, M[("hsT", icu, ct)])
                    for sl in range(2):
                        li = v.tensor_copy(hs_tm[:, sl * INTER + ct * 128: sl * INTER + ct * 128 + 128],
                                           pt0[:, sl * 128:(sl + 1) * 128])
                    li.then_inc(_vE, 1)
                    M[("hsTev", icu, ct)] = vE.inc()

                c0, c1 = g * 2, g * 2 + 1
                if pfx:
                    block.tensor(lambda t, ct=c0: _mm1(t, ct))
                    block.vector(lambda v, ct=c0: _evac1(v, ct))
                    block.tensor(lambda t, ct=c0: _conv1(t, ct))
                    block.tensor(lambda t, ct=c1: _mm1(t, ct))
                    block.vector(lambda v, ct=c1: _evac1(v, ct))
                    block.tensor(lambda t, ct=c1: _conv1(t, ct))
                else:
                    block.tensor(lambda t, ct=c0: _mm1(t, ct))
                    block.tensor(lambda t, ct=c1: _mm1(t, ct))
                    block.vector(lambda v, ct=c0: _evac1(v, ct))
                    block.tensor(lambda t, ct=c0: _conv1(t, ct))
                    block.vector(lambda v, ct=c1: _evac1(v, ct))
                    block.tensor(lambda t, ct=c1: _conv1(t, ct))
                block.scalar(lambda sc, ct=c0: _cvevac1(sc, ct))
                block.tensor(lambda t, ct=c0: _hsT1(t, ct))
                block.vector(lambda v, ct=c0: _hsTev1(v, ct))
                block.scalar(lambda sc, ct=c1: _cvevac1(sc, ct))
                block.tensor(lambda t, ct=c1: _hsT1(t, ct))
                block.vector(lambda v, ct=c1: _hsTev1(v, ct))

            def _bT(t):
                t.wait_ge(_aE, M[("cvac", icu, 32)])
                t.wait_ge(_vE, M[("hsTev", icu, 31)])
                for sl in range(2):
                    li = t.transpose(pt0[:, sl * 128:(sl + 1) * 128],
                                     bm_fm[:, sl * 128:sl * 128 + 128], iden)
                li.then_inc(_tE, 1)
                M[("bT", icu)] = tE.inc()

            def _bTev(v):
                v.wait_ge(_tE, M[("bT", icu)])
                for sl in range(2):
                    li = v.tensor_copy(bm_tm[:, sl * 128:(sl + 1) * 128],
                                       pt0[:, sl * 128:(sl + 1) * 128])
                li.then_inc(_vE, 1)
                M[("bTev", icu)] = vE.inc()

            block.tensor(_bT)
            block.vector(_bTev)

            for tsl in range(2):
                def _dtmm(t, tsl=tsl):
                    if icu == 0 and tsl == 0:
                        wq(t, M["wdtT"])
                    key = ("dtv", icu, tsl - 1) if tsl == 1 else \
                        (("dtv", icu - 1, 1) if icu > 0 else None)
                    if key and key in M:
                        t.wait_ge(_vE, M[key])
                    if tsl == 0 and icu > 0:
                        t.wait_ge(_vE, M[("acv", icu - 1)])
                    if pfx:
                        t.wait_ge(_vE, M[("xcev", icu, 33)])
                    for k in range(KT):
                        li = t.matmul(pb[4][:, 0:64],
                                      h_fm[:, k * 256 + tsl * 128: k * 256 + tsl * 128 + 128],
                                      wdtT[:, k * NH:(k + 1) * NH],
                                      start=(k == 0), stop=(k == KT - 1))
                    li.then_inc(_tE, 1)
                    M[("dtmm", icu, tsl)] = tE.inc()

                block.tensor(_dtmm)

                def _dtv(v, tsl=tsl):
                    v.wait_ge(_tE, M[("dtmm", icu, tsl)])
                    if icu == 0 and tsl == 0:
                        v.wait_ge(_vE, M["rows3v"])
                    v.tensor_add(dts1, pb[4][:, 0:64], dtbB).then_inc(_vE, 1)
                    M[("dts1", icu, tsl)] = vE.inc()

                block.vector(_dtv)

                def _dta(sc, tsl=tsl):
                    sc.wait_ge(_vE, M[("dts1", icu, tsl)])
                    ass(sc.activation(dts2, dts1, AF.Abs), sc)
                    ass(sc.activation(dts2, dts2, AF.Exp, scale=-1.0), sc)
                    sc.activation(dts2, dts2, AF.Ln, bias=1.0).then_inc(_aE, 1)
                    M[("dta", icu, tsl)] = aE.inc()

                block.scalar(_dta)

                def _dtv2(v, tsl=tsl):
                    v.wait_ge(_aE, M[("dta", icu, tsl)])
                    vss(v.scalar_tensor_tensor(dt_tm[:, tsl * 64:(tsl + 1) * 64], dts1, 0.0, dts2,
                                               ALU.max, ALU.add), v)
                    v.tensor_scalar(dt_tm[:, tsl * 64:(tsl + 1) * 64],
                                    dt_tm[:, tsl * 64:(tsl + 1) * 64], 1e-3, 100.0,
                                    ALU.max, ALU.min).then_inc(_vE, 1)
                    M[("dtclip", icu, tsl)] = vE.inc()
                    v.wait_ge(_vE, M[("dtclip", icu, tsl)])
                    v.tensor_mul(adt_tm[:, tsl * 64:(tsl + 1) * 64],
                                 dt_tm[:, tsl * 64:(tsl + 1) * 64], AB).then_inc(_vE, 1)
                    M[("dtv", icu, tsl)] = vE.inc()

                block.vector(_dtv2)

                if t_dt is not None:
                    def _tapdt(s, tsl=tsl):
                        g2 = icu * 2 + tsl
                        s.wait_ge(_vE, M[("dtv", icu, tsl)])
                        gi = QS.grp(s)
                        s.dma_start(out=t_dt[g2 * 128:(g2 + 1) * 128, :],
                                    in_=dt_tm[:, tsl * 64:(tsl + 1) * 64]).then_inc(QS.hs[gi], 16)
                        M[("dtap", icu, tsl)] = QS.tick(gi)

                    block.sync(_tapdt)

        r2S = (r2, r2B)
        e_sbS = (e_sb, e_sbB)
        mt_sbS = (mt_sb, mt_sbB)
        hs_ssdS = (hs_ssd, hs_ssdB)
        bdecS = (bdec, bdecB)
        s16tS = (s16t, s16tB)
        yoS = (yo, yoB)
        y1S = (y1, y1B)

        def emit_ssd(icu):
            def _ac(t):
                t.wait_ge(_vE, M[("dtv", icu, 1)])
                t.matmul(pb[4][:, 0:64], t01f, adt_tm[:, 0:64], start=True, stop=True)
                t.matmul(pb[4][:, 64:128], onesf, adt_tm[:, 0:64], start=True, stop=False)
                t.matmul(pb[4][:, 64:128], t01f, adt_tm[:, 64:128], start=False, stop=True)
                t.matmul(pb[5][0:1, 0:64], onesf[:, 0:1], adt_tm[:, 0:64], start=True, stop=False)
                t.matmul(pb[5][0:1, 0:64], onesf[:, 0:1], adt_tm[:, 64:128],
                         start=False, stop=True).then_inc(_tE, 1)
                M[("acmm", icu)] = tE.inc()

            block.tensor(_ac)

            def _acv(v):
                v.wait_ge(_tE, M[("acmm", icu)])
                v.tensor_copy(acum, pb[4][:, 0:128])
                v.tensor_copy(trow[:, icu * 64:(icu + 1) * 64], pb[5][0:1, 0:64]).then_inc(_vE, 1)
                M[("acv", icu)] = vE.inc()

            block.vector(_acv)

            def _acT(t):
                t.wait_ge(_vE, M[("acv", icu)])
                t.transpose(pb[4][0:64, 0:128], acum[:, 0:64], idf)
                t.transpose(pb[4][0:64, 128:256], acum[:, 64:128], idf)
                t.matmul(pb[5][:, 0:64], onesf[0:1, :], trow[:, icu * 64:(icu + 1) * 64],
                         start=True, stop=True).then_inc(_tE, 1)
                M[("acT", icu)] = tE.inc()

            block.tensor(_acT)

            def _acTv(v):
                v.wait_ge(_tE, M[("acT", icu)])
                v.tensor_copy(acum_hm[:, 0:128], pb[4][0:64, 0:128])
                v.tensor_copy(acum_hm[:, 128:256], pb[4][0:64, 128:256])
                vss(v.tensor_copy(tcB, pb[5][:, 0:64]), v)
                for zt in range(2):
                    li = v.tensor_sub(dsub[:, zt * 64:(zt + 1) * 64], tcB,
                                      acum[:, zt * 64:(zt + 1) * 64])
                li.then_inc(_vE, 1)
                M[("acd", icu)] = vE.inc()

            block.vector(_acTv)

            def _aca(sc):
                sc.wait_ge(_vE, M[("acd", icu)])
                sc.activation(etcB, tcB, AF.Exp)
                sc.activation(decayb, dsub, AF.Exp)
                for zt in range(2):
                    gsl = icu * 2 + zt
                    li = sc.activation(sdo[:, gsl * 64:(gsl + 1) * 64],
                                       acum[:, zt * 64:(zt + 1) * 64], AF.Exp)
                li.then_inc(_aE, 1)
                M[("aca", icu)] = aE.inc()

            block.scalar(_aca)

            def _gm(t):
                t.wait_ge(_vE, M[("acd", icu)])
                t.wait_ge(_aE, M[("cvac", icu, 33)])
                t.wait_ge(_vE, M[("bTev", icu)])
                for lt in range(2):
                    li = t.matmul(pb[5][:, lt * 256:(lt + 1) * 256],
                                  bm_fm[:, lt * 128:(lt + 1) * 128],
                                  cm_fm[:, icu * 256:(icu + 1) * 256], start=True, stop=True)
                li.then_inc(_tE, 1)
                M[("gmmm", icu)] = tE.inc()

            block.tensor(_gm)

            def _gmv(v):
                v.wait_ge(_tE, M[("gmmm", icu)])
                for lt in range(2):
                    li = v.tensor_mul(gmm[:, lt * 256:(lt + 1) * 256],
                                      pb[5][:, lt * 256:(lt + 1) * 256],
                                      mask01[:, lt * 256:(lt + 1) * 256])
                li.then_inc(_vE, 1)
                M[("gmv", icu)] = vE.inc()

            block.vector(_gmv)

            for h in range(NH):
                sl = h % 2
                bk = pb[sl]

                def _pe1(t, h=h, sl=sl, bk=bk):
                    if h < 2:
                        t.wait_ge(_vE, M[("gmv", icu)])
                        t.wait_ge(_aE, M[("aca", icu)])
                    else:
                        t.wait_ge(_vE, M[("ssdv", icu, h - 2)])
                    t.matmul(bk[:, 0:256], idf[0:64, h:h + 1].broadcast_to([64, 128]),
                             acum_hm[0:64, 0:256], start=True, stop=True).then_inc(_tE, 1)
                    M[("rowA", icu, h)] = tE.inc()

                block.tensor(_pe1)

                def _dv1(v, h=h, sl=sl, bk=bk):
                    v.wait_ge(_tE, M[("rowA", icu, h)])
                    for lt in range(2):
                        li = v.tensor_scalar(r2S[sl][:, lt * 256:(lt + 1) * 256], bk[:, 0:256],
                                             acum[:, lt * 64 + h:lt * 64 + h + 1], 0.0,
                                             ALU.subtract, ALU.min)
                    li.then_inc(_vE, 1)
                    M[("r2", icu, h)] = vE.inc()

                block.vector(_dv1)

                def _sc1(sc, h=h, sl=sl):
                    sc.wait_ge(_vE, M[("r2", icu, h)])
                    sc.activation(e_sbS[sl][:, 0:512], r2S[sl][:, 0:512],
                                  AF.Exp).then_inc(_aE, 1)
                    M[("eh", icu, h)] = aE.inc()

                block.scalar(_sc1)

                def _dv2(v, h=h, sl=sl):
                    v.wait_ge(_aE, M[("eh", icu, h)])
                    if h >= 2:
                        v.wait_ge(_tE, M[("mmh", icu, h - 2)])
                    else:
                        v.wait_ge(_tE, M[("gmmm", icu)])
                    for lt in range(2):
                        v.tensor_mul(mt_sbS[sl][:, lt * 256:(lt + 1) * 256],
                                     gmm[:, lt * 256:(lt + 1) * 256],
                                     e_sbS[sl][:, lt * 256:(lt + 1) * 256])
                        v.tensor_scalar(hs_ssdS[sl][:, lt * 64:(lt + 1) * 64],
                                        hs_tm[:, lt * INTER + h * 64: lt * INTER + h * 64 + 64],
                                        dt_tm[:, lt * 64 + h:lt * 64 + h + 1], None, ALU.mult)
                        v.tensor_scalar(bdecS[sl][:, lt * 128:(lt + 1) * 128],
                                        bm_tm[:, lt * 128:(lt + 1) * 128],
                                        decayb[:, lt * 64 + h:lt * 64 + h + 1], None, ALU.mult)
                    v.tensor_copy(s16tS[sl], S[:, h * HP:(h + 1) * HP]).then_inc(_vE, 1)
                    M[("mkv", icu, h)] = vE.inc()

                block.vector(_dv2)

                def _pe2(t, h=h, sl=sl, bk=bk):
                    t.wait_ge(_vE, M[("mkv", icu, h)])
                    for zt in range(2):
                        for lt in range(2):
                            t.matmul(bk[:, 256 + zt * 64:256 + (zt + 1) * 64],
                                     mt_sbS[sl][:, lt * 256 + zt * 128: lt * 256 + zt * 128 + 128],
                                     hs_ssdS[sl][:, lt * 64:(lt + 1) * 64],
                                     start=(lt == 0), stop=(lt == 1))
                    for lt in range(2):
                        t.matmul(bk[:, 384:448], bdecS[sl][:, lt * 128:(lt + 1) * 128],
                                 hs_ssdS[sl][:, lt * 64:(lt + 1) * 64],
                                 start=(lt == 0), stop=(lt == 1))
                    for zt in range(2):
                        li = t.matmul(bk[:, zt * 64:(zt + 1) * 64],
                                      cm_fm[:, icu * 256 + zt * 128: icu * 256 + zt * 128 + 128],
                                      s16tS[sl], start=True, stop=True)
                    li.then_inc(_tE, 1)
                    M[("mmh", icu, h)] = tE.inc()

                block.tensor(_pe2)

                def _dv3(v, h=h, sl=sl, bk=bk):
                    v.wait_ge(_tE, M[("mmh", icu, h)])
                    v.tensor_scalar(yoS[sl][:, 0:64], bk[:, 0:64],
                                    sdo[:, (icu * 2) * 64 + h:(icu * 2) * 64 + h + 1],
                                    None, ALU.mult)
                    vss(v.tensor_scalar(yoS[sl][:, 64:128], bk[:, 64:128],
                                        sdo[:, (icu * 2 + 1) * 64 + h:(icu * 2 + 1) * 64 + h + 1],
                                        None, ALU.mult), v)
                    v.tensor_add(y1S[sl][:, 0:64], bk[:, 256:320], yoS[sl][:, 0:64])
                    vss(v.tensor_add(y1S[sl][:, 64:128], bk[:, 320:384],
                                     yoS[sl][:, 64:128]), v)
                    for zt in range(2):
                        gsl = icu * 2 + zt
                        v.scalar_tensor_tensor(yb[:, gsl * INTER + h * 64: gsl * INTER + h * 64 + 64],
                                               hs_tm[:, zt * INTER + h * 64: zt * INTER + h * 64 + 64],
                                               DB[:, h:h + 1], y1S[sl][:, zt * 64:(zt + 1) * 64],
                                               ALU.mult, ALU.add)
                    v.scalar_tensor_tensor(S[:, h * HP:(h + 1) * HP], S[:, h * HP:(h + 1) * HP],
                                           etcB[:, h:h + 1], bk[:, 384:448],
                                           ALU.mult, ALU.add).then_inc(_vE, 1)
                    M[("ssdv", icu, h)] = vE.inc()

                block.vector(_dv3)

        for icu in range(NCH):
            for tsl in range(2):
                g = icu * 2 + tsl
                extra = [(_vE, M["bc_ca"]), (_vE, M["bc_sa"])] if g == 0 else []
                pef = M[("ssdfree", icu - 1)] if ("ssdfree", icu - 1) in M else None
                emit_hslice("hA", g, h_fm, 256, x_d[g * 128:(g + 1) * 128, :],
                            extra_v_waits=extra, pe_free_mark=None)
            M[("hfm_done", icu)] = vE.n
            emit_chunk_inproj(icu)
            emit_ssd(icu)
            M[("ssdfree", icu)] = vE.n
            if t_hs is not None:
                def _taphs(gx, icu=icu):
                    gx.wait_ge(_vE, M[("hsTev", icu, 31)])
                    gi = QG.grp(gx)
                    for sl in range(2):
                        g2 = icu * 2 + sl
                        gx.dma_start(out=t_hs[g2 * 128:(g2 + 1) * 128, :],
                                     in_=hs_tm[:, sl * INTER:(sl + 1) * INTER]).then_inc(QG.hs[gi], 16)
                        QG.tick(gi)
                    M[("taphs", icu)] = QG.mark(gi)

                block.gpsimd(_taphs)

                def _taphsw(v, icu=icu):
                    wq(v, M[("taphs", icu)])

                block.vector(_taphsw)
            if t_bc is not None:
                def _tapbc(g2, icu=icu):
                    g2.wait_ge(_vE, M[("bTev", icu)])
                    g2.wait_ge(_aE, M[("cvac", icu, 33)])
                    gi = QG.grp(g2)
                    g2.dma_start(out=t_bc[0:128, icu * 256:(icu + 1) * 256],
                                 in_=bm_fm[:, 0:256]).then_inc(QG.hs[gi], 16)
                    QG.tick(gi)
                    g2.dma_start(out=t_bc[128:256, icu * 256:(icu + 1) * 256],
                                 in_=cm_fm[:, icu * 256:(icu + 1) * 256]).then_inc(QG.hs[gi], 16)
                    M[("tapbc", icu)] = QG.tick(gi)

                block.gpsimd(_tapbc)

                def _tapbcw(v, icu=icu):
                    wq(v, M[("tapbc", icu)])

                block.vector(_tapbcw)


        if upto == "A":
            emit_stub_fin()
            return nc, tap_d

        def pB_v(v):
            v.wait_ge(_vE, M[("ssdv", NCH - 1, NH - 1)])
            wq(v, M["smalls"])
            v.tensor_scalar(sstage, S, msend_sb, None, ALU.mult).then_inc(_vE, 1)
            M["sstage"] = vE.inc()

        block.vector(pB_v)

        def pB_g(g):
            g.wait_ge(_vE, M["sstage"])
            gi = QG.grp(g)
            g.dma_start(out=ccin_st, in_=sstage).then_inc(QG.hs[gi], 16)
            wq(g, QG.tick(gi))
            g.collective_compute("AllReduce", ALU.add,
                                 replica_groups=[[0, 1], [2, 3], [4, 5], [6, 7]],
                                 ins=[ccin_st], outs=[ccout_st]).then_inc(_cc, 1)
            cc.inc()
            g.wait_ge(_cc, cc.n)
            gi = QG.grp(g)
            g.dma_start(out=sstage, in_=ccout_st).then_inc(QG.hs[gi], 16)
            M["srecv"] = QG.tick(gi)

        block.gpsimd(pB_g)

        def pB_v2(v):
            wq(v, M["srecv"])
            v.tensor_scalar(S, sstage, mrecv_sb, None, ALU.mult).then_inc(_vE, 1)
            M["Sin"] = vE.inc()

        block.vector(pB_v2)

        for c in range(NCH):
            if c > 0:
                def _ev(v, c=c):
                    if c > 1:
                        v.wait_ge(_vE, M[("evr", c - 1)])
                    v.tensor_add(evrow[0:1, 0:64], evrow[0:1, 0:64],
                                 trow[:, (c - 1) * 64:c * 64]).then_inc(_vE, 1)
                    M[("evr", c)] = vE.inc()

                def _ea(sc, c=c):
                    sc.wait_ge(_vE, M[("evr", c)])
                    sc.activation(evrow[0:1, 64:128], evrow[0:1, 0:64], AF.Exp).then_inc(_aE, 1)
                    M[("eva", c)] = aE.inc()

                def _et(t, c=c):
                    t.wait_ge(_aE, M[("eva", c)])
                    t.matmul(pb[4][:, 0:64], onesf[0:1, :], evrow[0:1, 64:128],
                             start=True, stop=True).then_inc(_tE, 1)
                    M[("evt", c)] = tE.inc()

                def _ev2(v, c=c):
                    v.wait_ge(_tE, M[("evt", c)])
                    v.tensor_copy(tcB, pb[4][:, 0:64]).then_inc(_vE, 1)
                    M[("evo", c)] = vE.inc()

                block.vector(_ev)
                block.scalar(_ea)
                block.tensor(_et)
                block.vector(_ev2)

            for h in range(NH):
                sl = h % 2
                bkB = pb[sl]

                def _cs(v, c=c, h=h, sl=sl):
                    if c > 0 and h < 2:
                        v.wait_ge(_vE, M[("evo", c)])
                    if h >= 2:
                        v.wait_ge(_tE, M[("cmm", c, h - 2)])
                    elif c > 0:
                        v.wait_ge(_tE, M[("cmm", c - 1, NH - 2 + h)])
                    if c == 0 and h == 0:
                        v.wait_ge(_vE, M["Sin"])
                    if c == 0:
                        v.tensor_copy(s16tS[sl], S[:, h * HP:(h + 1) * HP]).then_inc(_vE, 1)
                    else:
                        v.tensor_scalar(s16tS[sl], S[:, h * HP:(h + 1) * HP], tcB[:, h:h + 1],
                                        None, ALU.mult).then_inc(_vE, 1)
                    M[("cs16", c, h)] = vE.inc()

                def _cm(t, c=c, h=h, sl=sl, bkB=bkB):
                    t.wait_ge(_vE, M[("cs16", c, h)])
                    if h >= 2:
                        t.wait_ge(_vE, M[("corrv", c, h - 2)])
                    elif c > 0:
                        t.wait_ge(_vE, M[("corrv", c - 1, NH - 2 + h)])
                    for zt in range(2):
                        li = t.matmul(bkB[:, zt * 64:(zt + 1) * 64],
                                      cm_fm[:, c * 256 + zt * 128: c * 256 + zt * 128 + 128],
                                      s16tS[sl], start=True, stop=True)
                    li.then_inc(_tE, 1)
                    M[("cmm", c, h)] = tE.inc()

                def _cv(v, c=c, h=h, sl=sl, bkB=bkB):
                    v.wait_ge(_tE, M[("cmm", c, h)])
                    v.tensor_scalar(yoS[sl][:, 0:64], bkB[:, 0:64],
                                    sdo[:, (c * 2) * 64 + h:(c * 2) * 64 + h + 1],
                                    None, ALU.mult)
                    vss(v.tensor_scalar(yoS[sl][:, 64:128], bkB[:, 64:128],
                                        sdo[:, (c * 2 + 1) * 64 + h:(c * 2 + 1) * 64 + h + 1],
                                        None, ALU.mult), v)
                    for zt in range(2):
                        gsl = c * 2 + zt
                        slc = yb[:, gsl * INTER + h * 64: gsl * INTER + h * 64 + 64]
                        li = v.tensor_add(slc, slc, yoS[sl][:, zt * 64:(zt + 1) * 64])
                    li.then_inc(_vE, 1)
                    M[("corrv", c, h)] = vE.inc()

                block.vector(_cs)
                block.tensor(_cm)
                block.vector(_cv)

        M["ycorr"] = vE.n

        if t_y is not None:
            def _tapy(g, _ty=t_y):
                g.wait_ge(_vE, M["ycorr"])
                gi = QG.grp(g)
                for g2 in range(NTS):
                    g.dma_start(out=_ty[g2 * 128:(g2 + 1) * 128, :],
                                in_=yb[:, g2 * INTER:(g2 + 1) * INTER]).then_inc(QG.hs[gi], 16)
                    QG.tick(gi)
                M["tapy"] = QG.mark(gi)

            block.gpsimd(_tapy)

            def _tapyw(v):
                wq(v, M["tapy"])

            block.vector(_tapyw)

        if upto == "B":
            emit_stub_fin()
            return nc, tap_d

        def emit_bcast_dram(tag, dst, src_dram_col, ncols, add1=False, bank=6,
                            pre_t=(), pre_s=()):
            nj = (ncols + 511) // 512
            prev_ev = vE.n
            for j in range(nj):
                def _sy(s, j=j):
                    if j == 0:
                        for wk in pre_s:
                            s.wait_ge(wk[0], wk[1])
                    if j > 0:
                        s.wait_ge(_tE, M[(tag, "mm", j - 1)])
                    elif "rowst_free" in M:
                        s.wait_ge(_tE, M["rowst_free"])
                    gi = QS.grp(s)
                    s.dma_start(out=rowst, in_=src_dram_col[:, j * 512:(j + 1) * 512]).then_inc(QS.hs[gi], 16)
                    M[(tag, "ld", j)] = QS.tick(gi)

                def _t(t, j=j):
                    wq(t, M[(tag, "ld", j)])
                    if j == 0:
                        t.wait_ge(_vE, prev_ev)
                        for sm, val in pre_t:
                            t.wait_ge(sm, val)
                    if j >= 1:
                        t.wait_ge(_vE, M[(tag, "ev", j - 1)])
                    t.matmul(pb[bank][:, 0:512], onesf[0:1, :], rowst,
                             start=True, stop=True).then_inc(_tE, 1)
                    M[(tag, "mm", j)] = tE.inc()

                def _v(v, j=j):
                    v.wait_ge(_tE, M[(tag, "mm", j)])
                    if add1:
                        v.tensor_scalar(dst[:, j * 512:(j + 1) * 512], pb[bank][:, 0:512],
                                        1.0, None, ALU.add).then_inc(_vE, 1)
                    else:
                        v.tensor_copy(dst[:, j * 512:(j + 1) * 512], pb[bank][:, 0:512]).then_inc(_vE, 1)
                    M[(tag, "ev", j)] = vE.inc()

                block.sync(_sy)
                block.tensor(_t)
                block.vector(_v)
            M[tag] = vE.n
            M["rowst_free"] = M[(tag, "mm", nj - 1)]

        emit_bcast_dram("bc_ga", bcP2, mod_dram[:, 2 * HID:3 * HID], HID,
                        pre_s=[M["mod_dram"]])
        emit_bcast_dram("bc_nw", normwB, normw_d, INTER)

        for g in range(NTS):
            emit_hslice("hC", g, hfmC, T, x_d[g * 128:(g + 1) * 128, :])
        M["hC_done"] = vE.n

        if t_hc is not None:
            def _taphc(g2):
                g2.wait_ge(_vE, M["hC_done"])
                gi = QG.grp(g2)
                g2.dma_start(out=t_hc, in_=hfmC).then_inc(QG.hs[gi], 16)
                M["taphc"] = QG.tick(gi)

            block.gpsimd(_taphc)

            def _taphcw(t):
                wq(t, M["taphc"])

            block.tensor(_taphcw)

        for mg in range(16):
            buf = mg % 2
            tagw = ("wtgc", mg)

            def _stage(s, mg=mg, buf=buf, tagw=tagw):
                if mg >= 2:
                    s.wait_ge(_tE, M[("gmm", mg - 2, NTS - 1)])
                else:
                    s.wait_ge(_vE, M["ycorr"])
                r0 = mg * 256
                gi = QG.grp(s)
                for kq in range(4):
                    s.dma_start(out=wtgC[:, buf * 4096 + kq * 1024:
                                         buf * 4096 + (kq + 1) * 1024].rearrange(
                                    "p (k f) -> p k f", k=4),
                                in_=w_inT3[:, kq * 4:(kq + 1) * 4,
                                           r0:r0 + 256]).then_inc(QG.hs[gi], 16)
                    QG.tick(gi)
                M[tagw] = QG.mark(gi)

            block.gpsimd(_stage)

            for g in range(NTS):
                def _mm(t, g=g, mg=mg, buf=buf, tagw=tagw):
                    if g == 0:
                        wq(t, M[tagw])
                    if mg == 0 and g == 0:
                        t.wait_ge(_vE, M["hC_done"])
                    bk = pb[g % 2]
                    key = ("gac", mg, g - 2) if g >= 2 else (("gac", mg - 1, NTS - 2 + g) if mg > 0 else None)
                    if key:
                        t.wait_ge(_aE, M[key])
                    for k in range(KT):
                        li = t.matmul(bk[:, 0:256], hfmC[:, k * T + g * 128: k * T + g * 128 + 128],
                                      wtgC[:, buf * 4096 + k * 256:buf * 4096 + (k + 1) * 256],
                                      start=(k == 0), stop=(k == KT - 1))
                    li.then_inc(_tE, 1)
                    M[("gmm", mg, g)] = tE.inc()

                def _ac(sc, g=g, mg=mg):
                    bk = pb[g % 2]
                    sc.wait_ge(_tE, M[("gmm", mg, g)])
                    key = ("ygv", mg, g - 1) if g >= 1 else (("ygv", mg - 1, NTS - 1) if mg > 0 else None)
                    if key:
                        sc.wait_ge(_vE, M[key])
                    sc.activation(e_sb[:, 0:256], bk[:, 0:256], AF.Silu).then_inc(_aE, 1)
                    M[("gac", mg, g)] = aE.inc()

                def _yg(v, g=g, mg=mg):
                    v.wait_ge(_aE, M[("gac", mg, g)])
                    if mg == 0 and g == 0 and "tapy" in M:
                        wq(v, M["tapy"])
                    sl = yb[:, g * INTER + mg * 256: g * INTER + mg * 256 + 256]
                    vss(v.tensor_mul(sl, sl, e_sb[:, 0:256]), v)
                    v.scalar_tensor_tensor(scr2[:, 0:256], sl, 1.0, sl, ALU.mult, ALU.mult,
                                           accum_out=ssqp[:, g * 16 + mg:g * 16 + mg + 1]).then_inc(_vE, 1)
                    M[("ygv", mg, g)] = vE.inc()

                block.tensor(_mm)
                block.scalar(_ac)
                block.vector(_yg)

        M["yg_done"] = vE.n

        def _rms_v(v):
            v.wait_ge(_vE, M["yg_done"])
            for g in range(NTS):
                li = v.reduce_sum(st8[:, 8 + g:9 + g], ssqp[:, g * 16:(g + 1) * 16], axis=AX.X)
            li.then_inc(_vE, 1)
            M["ssq"] = vE.inc()

        def _rms_a(sc):
            sc.wait_ge(_vE, M["ssq"])
            ass(sc.activation(st8[:, 8:16], st8[:, 8:16], AF.Ln, scale=cst[:, 5:6],
                              bias=cst[:, 4:5]), sc)
            sc.activation(st8[:, 8:16], st8[:, 8:16], AF.Exp, scale=cst[:, 3:4]).then_inc(_aE, 1)
            M["rstd"] = aE.inc()

        block.vector(_rms_v)
        block.scalar(_rms_a)

        for g in range(NTS):
            def _yn(v, g=g):
                if g == 0:
                    v.wait_ge(_aE, M["rstd"])
                    v.wait_ge(_vE, M["bc_nw"])
                v.scalar_tensor_tensor(yb[:, g * INTER:(g + 1) * INTER],
                                       yb[:, g * INTER:(g + 1) * INTER],
                                       st8[:, 8 + g:9 + g], normwB, ALU.mult,
                                       ALU.mult).then_inc(_vE, 1)
                M[("ynv", g)] = vE.inc()

            block.vector(_yn)

        if t_yn is not None:
            def _tapyn(g):
                g.wait_ge(_vE, M[("ynv", NTS - 1)])
                gi = QG.grp(g)
                for g2 in range(NTS):
                    g.dma_start(out=t_yn[g2 * 128:(g2 + 1) * 128, :],
                                in_=yb[:, g2 * INTER:(g2 + 1) * INTER]).then_inc(QG.hs[gi], 16)
                    QG.tick(gi)
                M["tapyn"] = QG.mark(gi)

            block.gpsimd(_tapyn)

        for g in range(NTS):
            for q in range(KI // 4):
                def _t(t, g=g, q=q):
                    if g == 0 and q == 0:
                        t.wait_ge(_vE, M[("ynv", 0)])
                        if "tapyn" in M:
                            wq(t, M["tapyn"])
                    elif q == 0:
                        t.wait_ge(_vE, M[("ynv", g)])
                    if ("ynT_ev", g, q - 1) in M:
                        t.wait_ge(_vE, M[("ynT_ev", g, q - 1)])
                    elif g > 0 and ("ynT_ev", g - 1, KI // 4 - 1) in M:
                        t.wait_ge(_vE, M[("ynT_ev", g - 1, KI // 4 - 1)])
                    for kk in range(4):
                        kb = q * 4 + kk
                        li = t.transpose(pt0[:, kk * 128:(kk + 1) * 128],
                                         yb[:, g * INTER + kb * 128: g * INTER + kb * 128 + 128],
                                         iden)
                    li.then_inc(_tE, 1)
                    M[("ynT", g, q)] = tE.inc()

                def _v(v, g=g, q=q):
                    v.wait_ge(_tE, M[("ynT", g, q)])
                    for kk in range(4):
                        kb = q * 4 + kk
                        li = v.tensor_copy(ynf[:, kb * T + g * 128: kb * T + g * 128 + 128],
                                           pt0[:, kk * 128:(kk + 1) * 128])
                    li.then_inc(_vE, 1)
                    M[("ynT_ev", g, q)] = vE.inc()

                block.tensor(_t)
                block.vector(_v)
        M["ynf_done"] = vE.n

        for dg in range(8):
            buf = dg % 2
            tagw = ("wotg", dg)

            def _stage(s, dg=dg, buf=buf, tagw=tagw):
                if dg == 0:
                    s.wait_ge(_vE, M["ynf_done"])
                if dg >= 2:
                    s.wait_ge(_tE, M[("opmm", dg - 2, NTS - 1)])
                r0 = dg * 256
                gi = QG.grp(s)
                for iq in range(8):
                    s.dma_start(out=wot[:, buf * 8192 + iq * 1024:
                                        buf * 8192 + (iq + 1) * 1024].rearrange(
                                    "p (i f) -> p i f", i=4),
                                in_=w_outT3[:, iq * 4:(iq + 1) * 4,
                                            r0:r0 + 256]).then_inc(QG.hs[gi], 16)
                    QG.tick(gi)
                M[tagw] = QG.mark(gi)

            block.gpsimd(_stage)

            for g in range(NTS):
                def _xl(s, dg=dg, g=g):
                    slot = (dg * NTS + g) % 2
                    key = ("opv", dg, g - 2) if g >= 2 else (("opv", dg - 1, NTS - 2 + g) if dg > 0 else None)
                    if key and key in M:
                        s.wait_ge(_vE, M[key])
                    elif dg == 0:
                        s.wait_ge(_vE, M["hC_done"])
                    gi = QS.grp(s)
                    s.dma_start(out=xt[:, slot * 256:(slot + 1) * 256],
                                in_=x_d[g * 128:(g + 1) * 128, dg * 256:(dg + 1) * 256]).then_inc(QS.hs[gi], 16)
                    M[("opx", dg, g)] = QS.tick(gi)

                def _mm(t, dg=dg, g=g, buf=buf, tagw=tagw):
                    if g == 0:
                        wq(t, M[tagw])
                    bk = pb[g % 2]
                    key = ("opv", dg, g - 2) if g >= 2 else (("opv", dg - 1, NTS - 2 + g) if dg > 0 else None)
                    if key and key in M:
                        t.wait_ge(_vE, M[key])
                    for i in range(KI):
                        li = t.matmul(bk[:, 0:256], ynf[:, i * T + g * 128: i * T + g * 128 + 128],
                                      wot[:, buf * 8192 + i * 256:buf * 8192 + (i + 1) * 256],
                                      start=(i == 0), stop=(i == KI - 1))
                    li.then_inc(_tE, 1)
                    M[("opmm", dg, g)] = tE.inc()

                def _ev(v, dg=dg, g=g):
                    slot = (dg * NTS + g) % 2
                    v.wait_ge(_tE, M[("opmm", dg, g)])
                    wq(v, M[("opx", dg, g)])
                    idx = dg * NTS + g
                    pk = (idx - 2) // NTS, (idx - 2) % NTS
                    if idx >= 2 and ("opst", pk[0], pk[1]) in M:
                        wq(v, M[("opst", pk[0], pk[1])])
                    if dg == 0 and g == 0:
                        v.wait_ge(_vE, M["bc_ga"])
                    vss(v.tensor_mul(scr2[:, 256:512], pb[g % 2][:, 0:256],
                                     bcP2[:, dg * 256:(dg + 1) * 256]), v)
                    v.tensor_add(scr2[:, 512 + slot * 256:512 + (slot + 1) * 256],
                                 scr2[:, 256:512], xt[:, slot * 256:(slot + 1) * 256]).then_inc(_vE, 1)
                    M[("opv", dg, g)] = vE.inc()

                def _st(s, dg=dg, g=g):
                    slot = (dg * NTS + g) % 2
                    s.wait_ge(_vE, M[("opv", dg, g)])
                    gi = QS.grp(s)
                    s.dma_start(out=xnew_dram[g * 128:(g + 1) * 128, dg * 256:(dg + 1) * 256],
                                in_=scr2[:, 512 + slot * 256:512 + (slot + 1) * 256]).then_inc(QS.hs[gi], 16)
                    M[("opst", dg, g)] = QS.tick(gi)

                block.sync(_xl)
                block.tensor(_mm)
                block.vector(_ev)
                block.sync(_st)


        if upto == "C":
            def _fin(s):
                s.wait_ge(_vE, vE.n)
                s.wait_ge(_aE, aE.n)
                s.wait_ge(_tE, tE.n)
                for mk in QS.finals() + QG.finals():
                    wq(s, mk)
                gi = QS.grp(s)
                for g2 in range(NTS):
                    s.dma_start(out=out_d[g2 * 128:(g2 + 1) * 128, :],
                                in_=xnew_dram[g2 * 128:(g2 + 1) * 128, :]).then_inc(QS.hs[gi], 16)
                    QS.tick(gi)
                wq(s, QS.mark(gi))

            block.sync(_fin)
            return nc, tap_d

        emit_bcast_dram("bc_cb", bcP0, mod_dram[:, 4 * HID:5 * HID], HID, add1=True,
                        pre_t=[(_vE, M["hC_done"])])
        emit_bcast_dram("bc_sb", bcP1, mod_dram[:, 3 * HID:4 * HID], HID)
        emit_bcast_dram("bc_gb", bcP2, mod_dram[:, 5 * HID:6 * HID], HID,
                        pre_t=[(_vE, M[("opv", 7, NTS - 1)])])

        for g in range(NTS):
            def _w(v, g=g):
                if g == 0:
                    v.wait_ge(_vE, M["bc_cb"])
                    v.wait_ge(_vE, M["bc_sb"])

            block.vector(_w)
            emit_hslice("hD", g, h2f, T, xnew_dram[g * 128:(g + 1) * 128, :],
                        extra_s_waits=[M[("opst", dgx, g)] for dgx in range(8)])
        M["h2_done"] = vE.n

        for g in range(NTS):
            def _rt(t, g=g):
                if g == 0:
                    t.wait_ge(_vE, M["h2_done"])
                    wq(t, M["smalls"])
                if ("rtv", g - 1) in M:
                    t.wait_ge(_vE, M[("rtv", g - 1)])
                    t.wait_ge(_aE, M[("sig", g - 1)])
                for k in range(KT):
                    t.matmul(pb[4][:, 0:NEXP], h2f[:, k * T + g * 128: k * T + g * 128 + 128],
                             wr_sb[:, k * NEXP:(k + 1) * NEXP], start=(k == 0), stop=(k == KT - 1))
                for k in range(KT):
                    li = t.matmul(pb[4][:, 8:9], h2f[:, k * T + g * 128: k * T + g * 128 + 128],
                                  wsg_sb[:, k:k + 1], start=(k == 0), stop=(k == KT - 1))
                li.then_inc(_tE, 1)
                M[("rt", g)] = tE.inc()

            block.tensor(_rt)

            def _sm1(v, g=g):
                v.wait_ge(_tE, M[("rt", g)])
                v.reduce_max(st8[:, 0:1], pb[4][:, 0:NEXP], axis=AX.X).then_inc(_vE, 1)
                M[("sm1", g)] = vE.inc()

            def _sm2(sc, g=g):
                sc.wait_ge(_vE, M[("sm1", g)])
                ass(sc.activation(st8[:, 1:2], st8[:, 0:1], AF.Copy, scale=-1.0), sc)
                sc.activation(scr2[:, 0:8], pb[4][:, 0:NEXP], AF.Exp, bias=st8[:, 1:2])
                sc.activation(sig_sb[:, g:g + 1], pb[4][:, 8:9], AF.Sigmoid).then_inc(_aE, 1)
                M[("sig", g)] = aE.inc()

            def _sm3(v, g=g):
                v.wait_ge(_aE, M[("sig", g)])
                v.reduce_sum(st8[:, 2:3], scr2[:, 0:8], axis=AX.X).then_inc(_vE, 1)
                M[("smA", g)] = vE.inc()
                v.wait_ge(_vE, M[("smA", g)])
                v.reciprocal(st8[:, 3:4], st8[:, 2:3]).then_inc(_vE, 1)
                M[("smB", g)] = vE.inc()
                v.wait_ge(_vE, M[("smB", g)])
                v.tensor_scalar(scr2[:, 8:16], scr2[:, 0:8], st8[:, 3:4], None, ALU.mult).then_inc(_vE, 1)
                M[("smC", g)] = vE.inc()
                v.wait_ge(_vE, M[("smC", g)])
                v.reduce_max(st8[:, 4:5], scr2[:, 8:16], axis=AX.X).then_inc(_vE, 1)
                M[("smD", g)] = vE.inc()
                v.wait_ge(_vE, M[("smD", g)])
                v.tensor_scalar(scr2[:, 16:24], scr2[:, 8:16], st8[:, 4:5], None, ALU.is_ge).then_inc(_vE, 1)
                M[("smE", g)] = vE.inc()
                v.wait_ge(_vE, M[("smE", g)])
                v.scalar_tensor_tensor(scr2[:, 24:32], scr2[:, 16:24], -1e30, scr2[:, 8:16],
                                       ALU.mult, ALU.add).then_inc(_vE, 1)
                M[("smF", g)] = vE.inc()
                v.wait_ge(_vE, M[("smF", g)])
                v.reduce_max(st8[:, 5:6], scr2[:, 24:32], axis=AX.X).then_inc(_vE, 1)
                M[("smG", g)] = vE.inc()
                v.wait_ge(_vE, M[("smG", g)])
                v.tensor_scalar(scr2[:, 16:24], scr2[:, 8:16], st8[:, 5:6], None, ALU.is_ge).then_inc(_vE, 1)
                M[("smH", g)] = vE.inc()
                v.wait_ge(_vE, M[("smH", g)])
                v.tensor_mul(c_sb[:, g * 8:(g + 1) * 8], scr2[:, 8:16], scr2[:, 16:24]).then_inc(_vE, 1)
                M[("rtv", g)] = vE.inc()

            block.vector(_sm1)
            block.scalar(_sm2)
            block.vector(_sm3)

        if t_c is not None:
            def _tapc(s):
                s.wait_ge(_vE, M[("rtv", NTS - 1)])
                gi = QS.grp(s)
                for g2 in range(NTS):
                    s.dma_start(out=t_c[g2 * 128:(g2 + 1) * 128, :],
                                in_=c_sb[:, g2 * 8:(g2 + 1) * 8]).then_inc(QS.hs[gi], 16)
                    QS.tick(gi)

            block.sync(_tapc)

        blocks_ffn = [("E", e) for e in range(NEXP)] + [("S", gr) for gr in range(4)]
        NB = len(blocks_ffn)
        NFT = 11
        sd3 = sd_d.rearrange("(f p) d -> p f d", p=128)
        wd3 = [wd_d[e].rearrange("(f p) d -> p f d", p=128) for e in range(NEXP)]

        def gu_src(bi, ft):
            kind, idx = blocks_ffn[bi]
            if kind == "E":
                return wgT_d[idx][ft], wuT_d[idx][ft]
            return sgT_d[idx * NFT + ft], suT_d[idx * NFT + ft]

        def wd_src(bi):
            kind, idx = blocks_ffn[bi]
            if kind == "E":
                return wd3[idx], 0
            return sd3, idx * NFT

        def scale_col(bi, g):
            kind, idx = blocks_ffn[bi]
            if kind == "E":
                return c_sb[:, g * 8 + idx:g * 8 + idx + 1]
            return sig_sb[:, g:g + 1]

        def emit_gul(s):
            bi, ft = divmod(s, NFT)
            buf = s % 2
            wsrc, usrc = gu_src(bi, ft)

            def _ld(g2, s=s, buf=buf, wsrc=wsrc, usrc=usrc):
                if ("gumm_last", s - 2) in M:
                    g2.wait_ge(_tE, M[("gumm_last", s - 2)])
                else:
                    g2.wait_ge(_vE, M[("ynv", NTS - 1)])
                gi = QG.grp(g2)
                g2.dma_start(out=wgt2[:, buf * 2048:(buf + 1) * 2048],
                             in_=wsrc).then_inc(QG.hs[gi], 16)
                QG.tick(gi)
                g2.dma_start(out=wut2[:, buf * 2048:(buf + 1) * 2048],
                             in_=usrc).then_inc(QG.hs[gi], 16)
                M[("gul", s)] = QG.tick(gi)

            block.gpsimd(_ld)

        def emit_wdl(bi, dq):
            wsrc, fof = wd_src(bi)
            dst = wdq0 if dq % 2 == 0 else wdq1

            def _ld(sy, bi=bi, dq=dq, wsrc=wsrc, fof=fof, dst=dst):
                if dq >= 2:
                    sy.wait_ge(_tE, M[("dmm_last", bi, dq - 2)])
                elif bi > 0:
                    sy.wait_ge(_tE, M[("dmm_last", bi - 1, dq + 2)])
                elif dq == 0:
                    sy.wait_ge(_vE, M["ycorr"])
                else:
                    sy.wait_ge(_vE, M["h2_done"])
                gi = QG.grp(sy)
                for f0, f1 in ((0, 4), (4, 8), (8, 11)):
                    sy.dma_start(out=dst[:, f0 * 512:f1 * 512].rearrange(
                                     "p (f d) -> p f d", f=f1 - f0),
                                 in_=wsrc[:, fof + f0:fof + f1,
                                          dq * 512:(dq + 1) * 512]).then_inc(QG.hs[gi], 16)
                    QG.tick(gi)
                M[("wdl", bi, dq)] = QG.mark(gi)

            block.gpsimd(_ld)

        def emit_gu_step(s):
            bi, ft = divmod(s, NFT)
            buf = s % 2
            for tch in range(2):
                bkG = pb[0 if tch == 0 else 2]
                bkU = pb[1 if tch == 0 else 3]

                def _mm(t, s=s, ft=ft, buf=buf, tch=tch, bkG=bkG, bkU=bkU):
                    if tch == 0:
                        wq(t, M[("gul", s)])
                    if ("gua", s - 1, tch) in M:
                        t.wait_ge(_aE, M[("gua", s - 1, tch)])
                        t.wait_ge(_vE, M[("guv", s - 1, tch)])
                    elif s == 0 and tch == 0:
                        t.wait_ge(_vE, M["h2_done"])
                    for k in range(KT):
                        t.matmul(bkG[:, 0:512],
                                 wgt2[:, buf * 2048 + k * 128: buf * 2048 + (k + 1) * 128],
                                 h2f[:, k * T + tch * 512: k * T + (tch + 1) * 512],
                                 start=(k == 0), stop=(k == KT - 1))
                    for k in range(KT):
                        li = t.matmul(bkU[:, 0:512],
                                      wut2[:, buf * 2048 + k * 128: buf * 2048 + (k + 1) * 128],
                                      h2f[:, k * T + tch * 512: k * T + (tch + 1) * 512],
                                      start=(k == 0), stop=(k == KT - 1))
                    li.then_inc(_tE, 1)
                    M[("gumm", s, tch)] = tE.inc()
                    if tch == 1:
                        M[("gumm_last", s)] = tE.n

                block.tensor(_mm)

                def _ga(sc, s=s, tch=tch, bkG=bkG):
                    sc.wait_ge(_tE, M[("gumm", s, tch)])
                    if ("guv", s - 1, tch) in M:
                        sc.wait_ge(_vE, M[("guv", s - 1, tch)])
                    sc.activation(sgt[:, tch * 512:(tch + 1) * 512],
                                  bkG[:, 0:512], AF.Silu).then_inc(_aE, 1)
                    M[("gua", s, tch)] = aE.inc()

                block.scalar(_ga)

                def _gv(v, s=s, ft=ft, tch=tch, bkU=bkU, bi=bi):
                    v.wait_ge(_aE, M[("gua", s, tch)])
                    if ("dn_done", bi - 1) in M:
                        v.wait_ge(_tE, M[("dn_done", bi - 1)])
                    v.tensor_mul(gu[:, ft * T + tch * 512: ft * T + (tch + 1) * 512],
                                 bkU[:, 0:512],
                                 sgt[:, tch * 512:(tch + 1) * 512]).then_inc(_vE, 1)
                    M[("guv", s, tch)] = vE.inc()

                block.vector(_gv)

        def emit_down(bi):
            emit_wdl(bi, 0)
            emit_wdl(bi, 1)
            for dq in range(4):
                src = wdq0 if dq % 2 == 0 else wdq1
                for g in range(NTS):
                    ds = dq * NTS + g
                    bk = pb[4 + ds % 2]

                    def _mm(t, bi=bi, dq=dq, g=g, bk=bk, ds=ds, src=src):
                        if g == 0:
                            wq(t, M[("wdl", bi, dq)])
                        if dq == 0 and g == 0:
                            t.wait_ge(_vE, M[("guv", bi * NFT + NFT - 1, 1)])
                        if ds >= 2:
                            t.wait_ge(_vE, M[("dstt", bi, ds - 2)])
                        elif bi > 0:
                            t.wait_ge(_vE, M[("dstt", bi - 1, 30 + ds)])
                        for ftx in range(NFT):
                            li = t.matmul(bk[:, 0:512],
                                          gu[:, ftx * T + g * 128: ftx * T + g * 128 + 128],
                                          src[:, ftx * 512:(ftx + 1) * 512],
                                          start=(ftx == 0), stop=(ftx == NFT - 1))
                        li.then_inc(_tE, 1)
                        M[("dmm", bi, ds)] = tE.inc()
                        if g == NTS - 1:
                            M[("dmm_last", bi, dq)] = tE.n
                            if dq == 3:
                                M[("dn_done", bi)] = tE.n

                    block.tensor(_mm)

                    def _st(v, bi=bi, dq=dq, g=g, bk=bk, ds=ds):
                        v.wait_ge(_tE, M[("dmm", bi, ds)])
                        sl = moe[:, g * HID + dq * 512: g * HID + (dq + 1) * 512]
                        if bi == 0:
                            v.tensor_scalar(sl, bk[:, 0:512], scale_col(bi, g),
                                            None, ALU.mult).then_inc(_vE, 1)
                        else:
                            v.scalar_tensor_tensor(sl, bk[:, 0:512], scale_col(bi, g), sl,
                                                   ALU.mult, ALU.add).then_inc(_vE, 1)
                        M[("dstt", bi, ds)] = vE.inc()

                    block.vector(_st)
                if dq + 2 < 4:
                    emit_wdl(bi, dq + 2)

        for bi in range(NB):
            for ft in range(NFT):
                s = bi * NFT + ft
                emit_gul(s)
                emit_gu_step(s)
            emit_down(bi)
        for g in range(NTS):
            M[("moe_done", g)] = M[("dstt", NB - 1, 24 + g)]

        for g in range(NTS):
            def _xl(s, g=g):
                s.wait_ge(_vE, M[("moe_done", g)])
                s.wait_ge(_tE, M[("dn_done", NB - 1)])
                if ("fst", g - 1) in M:
                    wq(s, M[("fst", g - 1)])
                gi = QS.grp(s)
                s.dma_start(out=xt, in_=xnew_dram[g * 128:(g + 1) * 128, :]).then_inc(QS.hs[gi], 16)
                M[("fx", g)] = QS.tick(gi)

            def _fv(v, g=g):
                wq(v, M[("fx", g)])
                if ("fst", g - 1) in M:
                    wq(v, M[("fst", g - 1)])
                if g == 0:
                    v.wait_ge(_vE, M["bc_gb"])
                vss(v.tensor_mul(htm, moe[:, g * HID:(g + 1) * HID], bcP2), v)
                v.tensor_add(htm, htm, xt).then_inc(_vE, 1)
                M[("fv", g)] = vE.inc()

            def _fs(s, g=g):
                s.wait_ge(_vE, M[("fv", g)])
                gi = QS.grp(s)
                s.dma_start(out=out_d[g * 128:(g + 1) * 128, :], in_=htm).then_inc(QS.hs[gi], 16)
                M[("fst", g)] = QS.tick(gi)

            block.sync(_xl)
            block.vector(_fv)
            block.sync(_fs)

        def _wait_all(s):
            for mk in QS.finals() + QG.finals():
                wq(s, mk)

        block.sync(_wait_all)

    return nc, tap_d
